# version 01
# baseline (speedup 1.0000x reference)
import numpy as np
from contextlib import ExitStack

import concourse.bass as bass
import concourse.mybir as mybir
from concourse.bass_utils import run_bass_kernel_spmd

F32 = mybir.dt.float32
BF16 = mybir.dt.bfloat16
I32 = mybir.dt.int32
AF = mybir.ActivationFunctionType
ALU = mybir.AluOpType

D = 1024
T = 2048
MILESTONES = []
NKT = D // 128
LN_EPS = 1e-5

ENGINES = ("tensor", "vector", "scalar", "gpsimd", "sync")


class Prog:

    def __init__(self, nc, es):
        self.nc = nc
        self.es = es
        self.ops = {e: [] for e in ENGINES}
        self.sems = {}
        self.cnt = {}
        self._pp = None
        self._pl = None

    def sem(self, name):
        if name not in self.sems:
            self.sems[name] = self.es.enter_context(self.nc.semaphore(name))
            self.cnt[name] = 0
        return self.sems[name]

    def wait(self, eng, sname, val):
        if not sname or val is None or val <= 0:
            return
        s = self.sem(sname)
        self.ops[eng].append(lambda e, s=s, v=val: e.wait_ge(s, v))

    def run(self, eng, fn, inc=None, dma=False):
        if inc is not None:
            s = self.sem(inc)
            amt = 16 if dma else 1
            self.cnt[inc] += amt
            self.ops[eng].append(lambda e, fn=fn, s=s, a=amt: fn(e).then_inc(s, a))
            return self.cnt[inc]
        self.ops[eng].append(fn)
        return None

    def sb(self, name, shape, dtype):
        return self.nc.alloc_sbuf_tensor(name, shape, dtype).ap()

    def bank(self, i):
        if self._pp is None:
            self._pp = [self.nc.alloc_psum_tensor(f"ppool{i}", [128, 512], F32).ap()
                        for i in range(8)]
            self._pl = [("V", 0)] * 8
        return self._pp[i]

    def bank_wait(self, eng, i):
        self.bank(i)
        s, cv = self._pl[i]
        self.wait(eng, s, cv)

    def bank_consumed(self, i, sname, cv):
        self.bank(i)
        s0, c0 = self._pl[i]
        if s0 == sname:
            self._pl[i] = (sname, max(c0, cv))
        else:
            self._pl[i] = (sname, cv)

    def finalize(self, block):
        for eng in ENGINES:
            lst = self.ops[eng]
            if not lst:
                continue

            def runner(e, lst=lst):
                for f in lst:
                    f(e)

            getattr(block, eng)(runner)


def build_and_run(build_fn, in_maps, n_cores=8, **kw):
    nc = bass.Bass("TRN2", target_bir_lowering=False)
    with ExitStack() as es:
        p = Prog(nc, es)
        build_fn(p)
        with nc.Block() as block:
            p.finalize(block)
    res = run_bass_kernel_spmd(nc, in_maps, core_ids=list(range(n_cores)), **kw)
    return res.results



CONST_VALUES = {"ln_eps": LN_EPS, "inv127": 1.0 / 127.0, "eps6": 1e-6,
                "rms_eps": 1e-5, "half": 0.5, "inv127sq": 1.0 / (127.0 * 127.0)}


def make_common(p):
    c = {}
    c["ones_col_bf"] = p.sb("ones_col_bf", [128, 1], BF16)
    c["ones_row_f"] = p.sb("ones_row_f", [1, 128], F32)
    c["ones_row_bf"] = p.sb("ones_row_bf", [1, 128], BF16)
    p.run("vector", lambda e, a=c["ones_col_bf"]: e.memset(a, 1.0), inc="V")
    p.run("vector", lambda e, a=c["ones_row_f"]: e.memset(a, 1.0), inc="V")
    p.run("vector", lambda e, a=c["ones_row_bf"]: e.memset(a, 1.0), inc="V")
    cst = p.sb("consts", [128, len(CONST_VALUES)], F32)
    c["const_cols"] = {}
    for i, (nm, val) in enumerate(CONST_VALUES.items()):
        p.run("vector", lambda e, a=cst[:, i:i + 1], v=val: e.memset(a, v), inc="V")
        c["const_cols"][nm] = cst[:, i:i + 1]
    c["init_v"] = p.cnt["V"]
    p.wait("tensor", "V", c["init_v"])
    p.wait("scalar", "V", c["init_v"])
    c["cc"] = lambda nm, np_=128: c["const_cols"][nm][0:np_, :]
    return c



class LNBufs:
    def __init__(self, p, tag, need_xc=True):
        self.m_row = p.sb(f"{tag}_m", [1, T], F32)
        self.a_row = p.sb(f"{tag}_a", [1, T], F32)
        self.b_row = p.sb(f"{tag}_b", [1, T], F32)
        self.x2 = [p.sb(f"{tag}_x2_{i}", [128, 512], BF16) for i in range(2)]
        self.xc = [p.sb(f"{tag}_xc_{i}", [128, 512], BF16) for i in range(2)] if need_xc else None
        self.t1 = p.sb(f"{tag}_t1", [128, 512], F32)
        self.t2 = p.sb(f"{tag}_t2", [128, 512], F32)


def emit_ln(p, c, ln, src, out_cb, n_tiles=NKT, src_bf16=False,
            banks=(0, 1, 2, 3, 4, 5, 6, 7)):
    ones = c["ones_col_bf"]
    stage_free = [("T", 0)] * 4
    for i in range(8):
        p.bank_wait("tensor", i)
    for k in range(n_tiles):
        xt, xs, xc_ = src(k)
        for n in range(4):
            b = n % 2
            sl = slice(n * 512, (n + 1) * 512)
            if src_bf16:
                rhs_sum = xt[:, sl]
            else:
                p.wait("scalar", xs, xc_)
                p.wait("scalar", stage_free[b][0], stage_free[b][1])
                p.run("scalar", lambda e, o=ln.xc[b], i=xt[:, sl]:
                      e.activation(o, i, AF.Copy), inc="S")
                rhs_sum = ln.xc[b]
            p.wait("scalar", xs, xc_)
            p.wait("scalar", stage_free[2 + b][0], stage_free[2 + b][1])
            sc = p.run("scalar", lambda e, o=ln.x2[b], i=xt[:, sl]:
                       e.activation(o, i, AF.Square), inc="S")
            p.wait("tensor", "S", sc)
            if src_bf16:
                p.wait("tensor", xs, xc_)
            p.run("tensor", lambda e, o=p.bank(n)[0:1, :], l=ones, r=rhs_sum,
                  st=(k == 0), sp=(k == n_tiles - 1):
                  e.matmul(o, l, r, start=st, stop=sp), inc=None)
            tc = p.run("tensor", lambda e, o=p.bank(4 + n)[0:1, :], l=ones, r=ln.x2[b],
                       st=(k == 0), sp=(k == n_tiles - 1):
                       e.matmul(o, l, r, start=st, stop=sp), inc="T")
            stage_free[b] = ("T", tc)
            stage_free[2 + b] = ("T", tc)
    tdone = p.cnt["T"]
    dscale = 1.0 / (n_tiles * 128)
    for n in range(4):
        sl = slice(n * 512, (n + 1) * 512)
        p.wait("vector", "T", tdone)
        v1 = p.run("vector", lambda e, o=ln.m_row[:, sl], i=p.bank(n)[0:1, :]:
                   e.tensor_scalar_mul(o, i, dscale), inc="V")
        p.wait("scalar", "V", v1)
        s1 = p.run("scalar", lambda e, o=ln.b_row[:, sl], i=ln.m_row[:, sl]:
                   e.activation(o, i, AF.Square), inc="S")
        p.wait("vector", "S", s1)
        v2 = p.run("vector", lambda e, o=p.bank(n)[0:1, :], i=p.bank(4 + n)[0:1, :],
                   b=ln.b_row[:, sl]:
                   e.scalar_tensor_tensor(o, i, dscale, b, ALU.mult, ALU.subtract), inc="V")
        p.wait("scalar", "V", v2)
        s2 = p.run("scalar", lambda e, o=p.bank(4 + n)[0:1, :], i=p.bank(n)[0:1, :]:
                   e.activation(o, i, AF.Sqrt, bias=c["cc"]("ln_eps", 1)), inc="S")
        p.wait("vector", "S", s2)
        v3 = p.run("vector", lambda e, o=ln.a_row[:, sl], i=p.bank(4 + n)[0:1, :]:
                   e.reciprocal(o, i), inc="V")
        p.wait("vector", "V", v3)
        p.run("vector", lambda e, o=ln.b_row[:, sl], a=ln.m_row[:, sl], b=ln.a_row[:, sl]:
              e.tensor_mul(o, a, b), inc="V")
        p.bank_consumed(n, "S", s2)
        p.bank_consumed(4 + n, "V", v3)
    stats_v = p.cnt["V"]

    for h in range(2):
        bca, bcb = [], []
        for j in range(2):
            n = 2 * h + j
            sl = slice(n * 512, (n + 1) * 512)
            ba, bb = 4 + j, 6 + j
            p.bank_wait("tensor", ba)
            p.bank_wait("tensor", bb)
            p.wait("tensor", "V", stats_v)
            p.run("tensor", lambda e, o=p.bank(ba), l=c["ones_row_f"], r=ln.a_row[:, sl]:
                  e.matmul(o, l, r, start=True, stop=True), inc=None)
            tc = p.run("tensor", lambda e, o=p.bank(bb), l=c["ones_row_f"], r=ln.b_row[:, sl]:
                       e.matmul(o, l, r, start=True, stop=True), inc="T")
            bca.append(4 + j)
            bcb.append(6 + j)
        bc_t = p.cnt["T"]
        for k in range(n_tiles):
            xt, xs, xc_ = src(k)
            for j in range(2):
                n = 2 * h + j
                p.wait("vector", "T", bc_t)
                out_cb(k, n, xt, xs, xc_, p.bank(4 + j), p.bank(6 + j))
        for j in range(2):
            p.bank_consumed(4 + j, "V", p.cnt["V"])
            p.bank_consumed(6 + j, "V", p.cnt["V"])



class XStream:

    def __init__(self, p, tag, dram_ap, width=T, dtype=F32, nbuf=2):
        self.p = p
        self.tag = tag
        self.dram = dram_ap
        self.nbuf = nbuf
        self.bufs = [p.sb(f"{tag}{i}", [128, width], dtype) for i in range(nbuf)]
        self.free = [("V", 0)] * nbuf
        self.i = 0

    def load(self, k, eng="sync"):
        p = self.p
        b = self.i % self.nbuf
        self.i += 1
        s, cv = self.free[b]
        p.wait(eng, s, cv)
        sem = f"{self.tag}q{b}"
        g = p.run(eng, lambda e, o=self.bufs[b][:, :],
                  i=self.dram[k * 128:(k + 1) * 128, :]:
                  e.dma_start(out=o, in_=i), inc=sem, dma=True)
        return self.bufs[b], sem, g, b

    def consumed(self, b, sname, cv):
        s0, c0 = self.free[b]
        self.free[b] = (sname, cv if s0 != sname else max(c0, cv))




DHID0 = 4096
NMH = DHID0 // 128
TC0 = 512
NC0 = T // TC0

FFN0_STAGE = "full"


def build_ffn0(p):
    nc = p.nc
    xin = nc.dram_tensor("xin", [D, T], F32, kind="ExternalInput").ap()
    w1T = nc.dram_tensor("w1T", [D, DHID0], F32, kind="ExternalInput").ap()
    w2T = nc.dram_tensor("w2T", [D, DHID0], F32, kind="ExternalInput").ap()
    h1T = nc.dram_tensor("h1T", [DHID0, DHID0], F32, kind="ExternalInput").ap()
    h2T = nc.dram_tensor("h2T", [DHID0, DHID0], F32, kind="ExternalInput").ap()
    w3T = nc.dram_tensor("w3T", [DHID0, D], F32, kind="ExternalInput").ap()
    sc_in = nc.dram_tensor("sc", [128, 4 * NMH + NKT], F32, kind="ExternalInput").ap()
    xout = nc.dram_tensor("xout", [D, T], F32, kind="ExternalOutput").ap()

    c = make_common(p)
    ln = LNBufs(p, "ln")

    scales = p.sb("scales", [128, 4 * NMH + NKT], F32)
    sc_ds = p.run("sync", lambda e, o=scales[:, :], i=sc_in:
                  e.dma_start(out=o, in_=i), inc="DSC", dma=True)
    sget = {
        "s1": lambda m: scales[:, m:m + 1],
        "s2": lambda m: scales[:, NMH + m:NMH + m + 1],
        "sh1": lambda m: scales[:, 2 * NMH + m:2 * NMH + m + 1],
        "sh2": lambda m: scales[:, 3 * NMH + m:3 * NMH + m + 1],
        "s3": lambda m: scales[:, 4 * NMH + m:4 * NMH + m + 1],
    }

    xs = XStream(p, "xld", xin)
    xln = p.sb("xln", [128, NKT * T], BF16)
    h_a = p.sb("h_a", [128, NMH * TC0], BF16)
    h_b = p.sb("h_b", [128, NMH * TC0], BF16)
    ybuf = p.sb("ybuf", [128, NKT * T], BF16)
    strips = [p.sb(f"strip{i}", [128, 2 * NKT * 128], BF16) for i in range(2)]
    hstrips = [p.sb(f"hstrip{i}", [128, NMH * 128], BF16) for i in range(2)]
    tsil = p.sb("tsil", [128, 512], BF16)
    outb = [p.sb(f"outb{i}", [128, 512], F32) for i in range(2)]

    def src2(k):
        p.wait("sync", "S", p.cnt.get("S", 0))
        p.wait("sync", "V", p.cnt.get("V", 0))
        p.wait("sync", "T", p.cnt.get("T", 0))
        ap, sname, g, b = xs.load(k)
        return ap, sname, g

    def apply_pre(k, n, xt, xsname, xcnt, pa, pb_):
        sl = slice(n * 512, (n + 1) * 512)
        p.wait("vector", xsname, xcnt)
        p.wait("vector", "V", apply_pre.t1_free)
        v1 = p.run("vector", lambda e, o=ln.t1, i=xt[:, sl], b=pa:
                   e.tensor_mul(o, i, b), inc="V")
        p.wait("vector", "V", v1)
        v2 = p.run("vector", lambda e, o=xln[:, k * T:(k + 1) * T][:, sl], i=ln.t1, b=pb_:
                   e.tensor_sub(o, i, b), inc="V")
        apply_pre.t1_free = v2

    apply_pre.t1_free = 0

    emit_ln(p, c, ln, src2, apply_pre)
    xln_v = p.cnt["V"]

    strip_state = {"free": [("T", 0), ("T", 0)]}
    hstrip_state = {"free": [("T", 0), ("T", 0)]}

    hdump = None
    if FFN0_STAGE != "full":
        hdump = p.nc.dram_tensor("hdump", [DHID0, TC0], F32, kind="ExternalOutput").ap()

    def dump_hidden(buf):
        p.wait("gpsimd", "S", p.cnt["S"])
        p.wait("gpsimd", "V", p.cnt["V"])
        for m in range(NMH):
            p.run("gpsimd", lambda e, o=hdump[m * 128:(m + 1) * 128, :],
                  i=buf[:, m * TC0:(m + 1) * TC0]:
                  e.dma_start(out=o, in_=i), inc="DGD", dma=True)
        p.wait("gpsimd", "DGD", p.cnt["DGD"])

    for cidx in range(NC0 if FFN0_STAGE == "full" else 1):
        csl = slice(cidx * TC0, cidx * TC0 + TC0)

        for m in range(NMH):
            sb_i = m % 2
            pb1, pb2 = 2 * (m % 2), 2 * (m % 2) + 1
            s, cv = strip_state["free"][sb_i]
            p.wait("gpsimd", s, cv)
            src1 = w1T[:, m * 128:(m + 1) * 128].rearrange("(k q) j -> q k j", q=128)
            src2_ = w2T[:, m * 128:(m + 1) * 128].rearrange("(k q) j -> q k j", q=128)
            dst1 = strips[sb_i][:, 0:NKT * 128].rearrange("q (k j) -> q k j", k=NKT)
            dst2 = strips[sb_i][:, NKT * 128:2 * NKT * 128].rearrange("q (k j) -> q k j", k=NKT)
            p.run("gpsimd", lambda e, o=dst1, i=src1: e.dma_start(out=o, in_=i),
                  inc=f"WQ{sb_i}", dma=True)
            g = p.run("gpsimd", lambda e, o=dst2, i=src2_: e.dma_start(out=o, in_=i),
                      inc=f"WQ{sb_i}", dma=True)
            p.wait("tensor", f"WQ{sb_i}", g)
            p.wait("tensor", "V", xln_v)
            p.bank_wait("tensor", pb1)
            p.bank_wait("tensor", pb2)
            for k in range(NKT):
                p.run("tensor", lambda e, o=p.bank(pb1), l=strips[sb_i][:, k * 128:(k + 1) * 128],
                      r=xln[:, k * T:(k + 1) * T][:, csl], st=(k == 0), sp=(k == NKT - 1):
                      e.matmul(o, l, r, start=st, stop=sp), inc=None)
            tc = None
            for k in range(NKT):
                tc = p.run("tensor", lambda e, o=p.bank(pb2),
                      l=strips[sb_i][:, NKT * 128 + k * 128:NKT * 128 + (k + 1) * 128],
                      r=xln[:, k * T:(k + 1) * T][:, csl], st=(k == 0), sp=(k == NKT - 1):
                      e.matmul(o, l, r, start=st, stop=sp),
                      inc="T" if k == NKT - 1 else None)
            strip_state["free"][sb_i] = ("T", tc)
            p.wait("scalar", "T", tc)
            p.wait("scalar", "DSC", sc_ds)
            p.wait("scalar", "V", p.cnt["V"])
            ssc = p.run("scalar", lambda e, o=tsil, i=p.bank(pb1), s=sget["s1"](m):
                        e.activation(o, i, AF.Silu, scale=s), inc="S")
            p.wait("vector", "S", ssc)
            vc = p.run("vector", lambda e, o=h_a[:, m * TC0:(m + 1) * TC0], i=p.bank(pb2),
                       s=sget["s2"](m), t=tsil:
                       e.scalar_tensor_tensor(o, i, s, t, ALU.mult, ALU.mult), inc="V")
            p.bank_consumed(pb1, "S", ssc)
            p.bank_consumed(pb2, "V", vc)
        src_ready = ("V", p.cnt["V"])

        if FFN0_STAGE == "w12":
            dump_hidden(h_a)
            return

        phases = (
            (h1T, NMH, NMH, h_a, "hb", AF.Gelu, "sh1"),
            (h2T, NMH, NMH, h_b, "ha", AF.Gelu, "sh2"),
            (w3T, NMH, NKT, h_a, "y", AF.Copy, "s3"),
        )
        if FFN0_STAGE == "h1":
            phases = phases[:1]
        elif FFN0_STAGE == "h2":
            phases = phases[:2]
        for (wT, n_k, n_m, src_buf, dst_kind, actf, sname) in phases:
            for m in range(n_m):
                sb_i = m % 2
                pb = 2 * (m % 2)
                s, cv = hstrip_state["free"][sb_i]
                p.wait("gpsimd", s, cv)
                srcw = wT[:, m * 128:(m + 1) * 128].rearrange("(k q) j -> q k j", q=128)
                dstw = hstrips[sb_i].rearrange("q (k j) -> q k j", k=NMH)
                g = p.run("gpsimd", lambda e, o=dstw, i=srcw: e.dma_start(out=o, in_=i),
                          inc=f"WQ{sb_i}", dma=True)
                p.wait("tensor", f"WQ{sb_i}", g)
                p.wait("tensor", src_ready[0], src_ready[1])
                p.bank_wait("tensor", pb)
                tc = None
                for k in range(n_k):
                    tc = p.run("tensor", lambda e, o=p.bank(pb),
                          l=hstrips[sb_i][:, k * 128:(k + 1) * 128],
                          r=src_buf[:, k * TC0:(k + 1) * TC0], st=(k == 0), sp=(k == n_k - 1):
                          e.matmul(o, l, r, start=st, stop=sp),
                          inc="T" if k == n_k - 1 else None)
                hstrip_state["free"][sb_i] = ("T", tc)
                p.wait("scalar", "T", tc)
                if dst_kind == "y":
                    o_ap = ybuf[:, m * T:(m + 1) * T][:, csl]
                elif dst_kind == "hb":
                    o_ap = h_b[:, m * TC0:(m + 1) * TC0]
                else:
                    o_ap = h_a[:, m * TC0:(m + 1) * TC0]
                ssc = p.run("scalar", lambda e, o=o_ap, i=p.bank(pb), s=sget[sname](m), a=actf:
                            e.activation(o, i, a, scale=s), inc="S")
                p.bank_consumed(pb, "S", ssc)
            src_ready = ("S", p.cnt["S"])
        if FFN0_STAGE == "h1":
            dump_hidden(h_b)
            return
        if FFN0_STAGE == "h2":
            dump_hidden(h_a)
            return

    y_s = p.cnt["S"]

    def src_y(k):
        return ybuf[:, k * T:(k + 1) * T], "S", y_s

    xs2 = xs

    def apply_post(k, n, yt, ysname, ycnt, pa, pb_):
        sl = slice(n * 512, (n + 1) * 512)
        if n % 2 == 0:
            p.wait("sync", "V", p.cnt["V"])
            p.wait("sync", "S", p.cnt["S"])
            p.wait("sync", "T", p.cnt["T"])
            ap, sname, g, b = xs2.load(k)
            apply_post.res = (ap, sname, g)
        rap, rs, rc = apply_post.res
        ob = outb[k % 2]
        p.wait("vector", ysname, ycnt)
        p.wait("vector", "V", apply_post.t_free)
        v1 = p.run("vector", lambda e, o=ln.t1, i=yt[:, sl], b=pa:
                   e.tensor_mul(o, i, b), inc="V")
        p.wait("vector", "V", v1)
        v2 = p.run("vector", lambda e, o=ln.t2, i=ln.t1, b=pb_:
                   e.tensor_sub(o, i, b), inc="V")
        p.wait("vector", rs, rc)
        p.wait("vector", f"OQ{k % 2}", apply_post.ob_free.get(k % 2, 0))
        p.wait("vector", "V", v2)
        v3 = p.run("vector", lambda e, o=ob[:, :], a=ln.t2, b=rap[:, sl]:
                   e.tensor_add(o, a, b), inc="V")
        apply_post.t_free = v3
        p.wait("sync", "V", v3)
        dcnt = p.run("sync", lambda e, o=xout[k * 128:(k + 1) * 128, sl], i=ob[:, :]:
                     e.dma_start(out=o, in_=i), inc=f"OQ{k % 2}", dma=True)
        apply_post.ob_free[k % 2] = dcnt

    apply_post.t_free = 0
    apply_post.ob_free = {}
    apply_post.res = None

    emit_ln(p, c, ln, src_y, apply_post, src_bf16=True)
    p.wait("sync", "OQ0", p.cnt.get("OQ0", 0))
    p.wait("sync", "OQ1", p.cnt.get("OQ1", 0))



NH0 = 16
DK0 = 64
INV8 = 0.125
VEC = 66
PTROWS = 319


def build_attn0_pre(p):
    nc = p.nc
    xin = nc.dram_tensor("xin", [D, T], F32, kind="ExternalInput").ap()
    WqT = nc.dram_tensor("WqT", [D, D], F32, kind="ExternalInput").ap()
    WkT = nc.dram_tensor("WkT", [D, D], F32, kind="ExternalInput").ap()
    WvT = nc.dram_tensor("WvT", [D, D], F32, kind="ExternalInput").ap()
    rkT = nc.dram_tensor("rkT", [DK0, 65], F32, kind="ExternalInput").ap()
    qtd = nc.dram_tensor("qtd", [128, NKT * T], BF16, kind="ExternalOutput").ap()
    ktd = nc.dram_tensor("ktd", [128, NKT * T], BF16, kind="ExternalOutput").ap()
    vxd = nc.dram_tensor("vxd", [128, 16 * NH0 * VEC], BF16, kind="ExternalOutput").ap()
    cnegd = nc.dram_tensor("cnegd", [1, 1], F32, kind="ExternalOutput").ap()

    c = make_common(p)
    ln = LNBufs(p, "ln")

    xs = XStream(p, "xld", xin)
    xln = p.sb("xln", [128, NKT * T], BF16)
    QT = p.sb("QT", [128, NKT * T], BF16)
    KT = p.sb("KT", [128, NKT * T], BF16)
    Vx = p.sb("Vx", [128, 16 * NH0 * VEC], BF16)
    Wv_sb = p.sb("Wv_sb", [128, NKT * D], BF16)
    strips = [p.sb(f"strip{i}", [128, 2 * NKT * 128], BF16) for i in range(2)]
    rk_sb = p.sb("rk_sb", [128, 65], BF16)
    cv_sb = p.sb("cv_sb", [1, 8], F32)

    def src2(k):
        p.wait("sync", "S", p.cnt.get("S", 0))
        p.wait("sync", "V", p.cnt.get("V", 0))
        p.wait("sync", "T", p.cnt.get("T", 0))
        ap, sname, g, b = xs.load(k)
        return ap, sname, g

    def apply_pre(k, n, xt, xsname, xcnt, pa, pb_):
        sl = slice(n * 512, (n + 1) * 512)
        p.wait("vector", xsname, xcnt)
        p.wait("vector", "V", apply_pre.t1_free)
        v1 = p.run("vector", lambda e, o=ln.t1, i=xt[:, sl], b=pa:
                   e.tensor_mul(o, i, b), inc="V")
        p.wait("vector", "V", v1)
        v2 = p.run("vector", lambda e, o=xln[:, k * T:(k + 1) * T][:, sl], i=ln.t1, b=pb_:
                   e.tensor_sub(o, i, b), inc="V")
        apply_pre.t1_free = v2

    apply_pre.t1_free = 0
    emit_ln(p, c, ln, src2, apply_pre)
    xln_v = p.cnt["V"]

    p.run("gpsimd", lambda e, o=Wv_sb.rearrange("q (k j) -> q k j", k=NKT),
          i=WvT.rearrange("(k q) j -> q k j", q=128):
          e.dma_start(out=o, in_=i), inc="WV", dma=True)
    p.run("gpsimd", lambda e, o=rk_sb[0:64, :], i=rkT: e.dma_start(out=o, in_=i),
          inc="WV", dma=True)
    gw = p.run("gpsimd", lambda e, o=rk_sb[64:128, :], i=rkT: e.dma_start(out=o, in_=i),
               inc="WV", dma=True)

    strip_state = {"free": [("T", 0), ("T", 0)]}
    for m in range(NKT):
        sb_i = m % 2
        s, cv = strip_state["free"][sb_i]
        p.wait("gpsimd", s, cv)
        src1 = WqT[:, m * 128:(m + 1) * 128].rearrange("(k q) j -> q k j", q=128)
        src2_ = WkT[:, m * 128:(m + 1) * 128].rearrange("(k q) j -> q k j", q=128)
        dst1 = strips[sb_i][:, 0:NKT * 128].rearrange("q (k j) -> q k j", k=NKT)
        dst2 = strips[sb_i][:, NKT * 128:2 * NKT * 128].rearrange("q (k j) -> q k j", k=NKT)
        p.run("gpsimd", lambda e, o=dst1, i=src1: e.dma_start(out=o, in_=i),
              inc=f"WQ{sb_i}", dma=True)
        g = p.run("gpsimd", lambda e, o=dst2, i=src2_: e.dma_start(out=o, in_=i),
              inc=f"WQ{sb_i}", dma=True)
        p.wait("tensor", f"WQ{sb_i}", g)
        p.wait("tensor", "V", xln_v)
        tc = None
        for n in range(4):
            nsl = slice(n * 512, (n + 1) * 512)
            pb1, pb2 = 2 * (n % 2), 2 * (n % 2) + 1
            p.bank_wait("tensor", pb1)
            p.bank_wait("tensor", pb2)
            for k in range(NKT):
                p.run("tensor", lambda e, o=p.bank(pb1), l=strips[sb_i][:, k * 128:(k + 1) * 128],
                      r=xln[:, k * T:(k + 1) * T][:, nsl], st=(k == 0), sp=(k == NKT - 1):
                      e.matmul(o, l, r, start=st, stop=sp), inc=None)
            for k in range(NKT):
                tc = p.run("tensor", lambda e, o=p.bank(pb2),
                      l=strips[sb_i][:, NKT * 128 + k * 128:NKT * 128 + (k + 1) * 128],
                      r=xln[:, k * T:(k + 1) * T][:, nsl], st=(k == 0), sp=(k == NKT - 1):
                      e.matmul(o, l, r, start=st, stop=sp),
                      inc="T" if k == NKT - 1 else None)
            p.wait("scalar", "T", tc)
            s1 = p.run("scalar", lambda e, o=QT[:, m * T:(m + 1) * T][:, nsl], i=p.bank(pb1):
                       e.activation(o, i, AF.Copy, scale=INV8), inc="S")
            s2 = p.run("scalar", lambda e, o=KT[:, m * T:(m + 1) * T][:, nsl], i=p.bank(pb2):
                       e.activation(o, i, AF.Copy), inc="S")
            p.bank_consumed(pb1, "S", s1)
            p.bank_consumed(pb2, "S", s2)
        strip_state["free"][sb_i] = ("T", tc)

    vx_ones = Vx.rearrange("q (t h z) -> q t h z", t=16, h=NH0)[:, :, :, 64:66]
    p.run("vector", lambda e, o=vx_ones: e.memset(o, 1.0), inc="V")
    p.wait("tensor", "WV", gw)
    for tt in range(16):
        for n2 in range(2):
            pb = 4 + (n2 % 2) + 2 * (tt % 2)
            p.bank_wait("tensor", pb)
            tc = None
            for k in range(NKT):
                tc = p.run("tensor", lambda e, o=p.bank(pb),
                      l=xln[:, k * T:(k + 1) * T][:, tt * 128:(tt + 1) * 128],
                      r=Wv_sb[:, k * D + n2 * 512:k * D + (n2 + 1) * 512],
                      st=(k == 0), sp=(k == NKT - 1):
                      e.matmul(o, l, r, start=st, stop=sp),
                      inc="T" if k == NKT - 1 else None)
            p.wait("scalar", "T", tc)
            sc = None
            for hh in range(8):
                h = n2 * 8 + hh
                sc = p.run("scalar", lambda e,
                           o=Vx[:, tt * NH0 * VEC + h * VEC:tt * NH0 * VEC + h * VEC + 64],
                           i=p.bank(pb)[:, hh * 64:(hh + 1) * 64]:
                           e.activation(o, i, AF.Copy), inc="S")
            p.bank_consumed(pb, "S", sc)

    qmax = ln.m_row
    kmax = ln.a_row
    p.run("vector", lambda e, o=qmax: e.memset(o, 0.0), inc="V")
    vz = p.run("vector", lambda e, o=kmax: e.memset(o, 0.0), inc="V")
    sq = ln.x2[0]
    for which, buf, mrow in (("q", QT, qmax), ("k", KT, kmax)):
        for kt in range(NKT):
            for n in range(4):
                nsl = slice(n * 512, (n + 1) * 512)
                pb = 6
                p.wait("scalar", "T", p.cnt["T"])
                sc = p.run("scalar", lambda e, o=sq, i=buf[:, kt * T:(kt + 1) * T][:, nsl]:
                           e.activation(o, i, AF.Square), inc="S")
                p.wait("tensor", "S", sc)
                p.bank_wait("tensor", pb)
                tc = p.run("tensor", lambda e, o=p.bank(pb)[0:1, :], l=c["ones_col_bf"], r=sq:
                           e.matmul(o, l, r, start=True, stop=True), inc="T")
                p.wait("vector", "T", tc)
                p.wait("vector", "V", vz)
                vc = p.run("vector", lambda e, o=mrow[:, nsl], a=mrow[:, nsl], b=p.bank(pb)[0:1, :]:
                           e.tensor_max(o, a, b), inc="V")
                p.bank_consumed(pb, "V", vc)
    p.wait("scalar", "WV", gw)
    p.wait("scalar", "T", p.cnt["T"])
    sc = p.run("scalar", lambda e, o=sq[0:64, 0:65], i=rk_sb[0:64, :]:
               e.activation(o, i, AF.Square), inc="S")
    p.wait("tensor", "S", sc)
    p.bank_wait("tensor", 7)
    tc = p.run("tensor", lambda e, o=p.bank(7)[0:1, 0:65], l=c["ones_col_bf"][0:64, :],
               r=sq[0:64, 0:65]: e.matmul(o, l, r, start=True, stop=True), inc="T")
    p.wait("vector", "T", tc)
    p.wait("vector", "V", p.cnt["V"])
    v1 = p.run("vector", lambda e, o=cv_sb[:, 0:1], i=qmax:
               e.reduce_max(o, i, mybir.AxisListType.X), inc="V")
    v2 = p.run("vector", lambda e, o=cv_sb[:, 1:2], i=kmax:
               e.reduce_max(o, i, mybir.AxisListType.X), inc="V")
    v3 = p.run("vector", lambda e, o=cv_sb[:, 2:3], i=p.bank(7)[0:1, 0:65]:
               e.reduce_max(o, i, mybir.AxisListType.X), inc="V")
    p.bank_consumed(7, "V", v3)
    p.wait("scalar", "V", v3)
    s1 = p.run("scalar", lambda e, o=cv_sb[:, 4:7], i=cv_sb[:, 0:3]:
               e.activation(o, i, AF.Sqrt), inc="S")
    p.wait("vector", "S", s1)
    v4 = p.run("vector", lambda e, o=cv_sb[:, 7:8], a=cv_sb[:, 5:6], b=cv_sb[:, 6:7]:
               e.tensor_add(o, a, b), inc="V")
    p.wait("vector", "V", v4)
    v5 = p.run("vector", lambda e, o=cv_sb[:, 3:4], a=cv_sb[:, 4:5], b=cv_sb[:, 7:8]:
               e.tensor_mul(o, a, b), inc="V")
    p.wait("vector", "V", v5)
    v6 = p.run("vector", lambda e, o=cv_sb[:, 3:4], i=cv_sb[:, 3:4]:
               e.tensor_scalar_mul(o, i, -1.0), inc="V")

    p.wait("sync", "V", v6)
    p.wait("sync", "S", p.cnt["S"])
    p.run("sync", lambda e, o=cnegd, i=cv_sb[:, 3:4]: e.dma_start(out=o, in_=i),
          inc="OQ0", dma=True)
    p.run("sync", lambda e, o=qtd, i=QT[:, :]: e.dma_start(out=o, in_=i), inc="OQ0", dma=True)
    p.run("sync", lambda e, o=ktd, i=KT[:, :]: e.dma_start(out=o, in_=i), inc="OQ0", dma=True)
    p.run("sync", lambda e, o=vxd, i=Vx[:, :]: e.dma_start(out=o, in_=i), inc="OQ0", dma=True)
    p.wait("sync", "OQ0", p.cnt["OQ0"])



PHROW = T + 64


def build_attn0_core(p):
    nc = p.nc
    xin = nc.dram_tensor("xin", [D, T], F32, kind="ExternalInput").ap()
    qtd = nc.dram_tensor("qtd", [128, NKT * T], BF16, kind="ExternalInput").ap()
    ktd = nc.dram_tensor("ktd", [128, NKT * T], BF16, kind="ExternalInput").ap()
    vxd = nc.dram_tensor("vxd", [128, 16 * NH0 * VEC], BF16, kind="ExternalInput").ap()
    cnegd = nc.dram_tensor("cnegd", [1, 1], F32, kind="ExternalInput").ap()
    rkT = nc.dram_tensor("rkT", [DK0, 65], F32, kind="ExternalInput").ap()
    rv = nc.dram_tensor("rv", [65, DK0], F32, kind="ExternalInput").ap()
    iden = nc.dram_tensor("iden", [128, 128], F32, kind="ExternalInput").ap()
    WoT = nc.dram_tensor("WoT", [D, D], F32, kind="ExternalInput").ap()
    bo = nc.dram_tensor("bo", [128, NKT], F32, kind="ExternalInput").ap()
    xout = nc.dram_tensor("xout", [D, T], F32, kind="ExternalOutput").ap()
    ppad = nc.dram_tensor("ppad", [NH0 * PHROW + 64, 319], BF16, kind="Internal").ap()
    wpad = nc.dram_tensor("wpad", [PHROW + 64, 319], BF16, kind="Internal").ap()
    e2d = nc.dram_tensor("e2d", [2, DK0], BF16, kind="Internal").ap()

    c = make_common(p)
    ln = LNBufs(p, "ln", need_xc=False)

    QT = p.sb("QT", [128, NKT * T], BF16)
    KT_h = p.sb("KT_h", [128, T], BF16)
    Vx = p.sb("Vx", [128, 16 * NH0 * VEC], BF16)
    ctx = p.sb("ctx", [128, NKT * T], BF16)
    ppst_v = ctx[:, 0:16 * 319]
    expT = [p.sb(f"expT{i}", [128, T], BF16) for i in range(2)]
    iden_sb = p.sb("iden_sb", [128, 128], BF16)
    rk_sb = p.sb("rk_sb", [128, 65], BF16)
    ones127 = p.sb("ones127", [128, 127], BF16)
    ppst = None
    pt0t = p.sb("pt0t", [1, T], BF16)
    pt64t = p.sb("pt64t", [1, T], BF16)
    apc1 = p.sb("apc1", [128, 16 * 128], BF16)
    apc2 = p.sb("apc2", [64, 16 * 128], BF16)
    bnd_sb = p.sb("bnd_sb", [128, 192], BF16)
    xst1 = p.sb("xst1", [128, 16 * 128], BF16)
    xst2 = p.sb("xst2", [64, 16 * 128], BF16)
    wim = p.sb("wim", [128, 16 * 65], BF16)
    WunX = p.sb("WunX", [65, T], BF16)
    dwf = p.sb("dwf", [66, T], BF16)
    rdenf = p.sb("rdenf", [65, T], BF16)
    rdenb = ln.t1.bitcast(BF16)[:, 0:512]
    cbias = p.sb("cbias", [128, 1], F32)
    cneg_sb = p.sb("cneg_sb", [1, 1], F32)
    rv_f = p.sb("rv_f", [65, DK0], F32)
    t64 = p.sb("t64", [1, DK0], F32)
    embXb = p.sb("embXb", [65, DK0], BF16)
    e_dw66 = p.sb("e_dw66", [66, DK0], BF16)
    onesf2 = p.sb("onesf2", [65, 128], BF16)
    r01 = p.sb("r01", [1, 2 * DK0], BF16)
    zrow = ln.t2.bitcast(BF16)[0:1, 0:319]
    bo_sb = p.sb("bo_sb", [128, NKT], F32)
    zvec = p.sb("zvec", [128, 66], BF16)
    ctxu_sb = p.sb("ctxu_sb", [64, 512], F32)
    strips = [p.sb(f"strip{i}", [128, NKT * 128], BF16) for i in range(2)]
    outb = [p.sb("outb0", [128, 512], F32)]
    ybuf = QT

    p.run("sync", lambda e, o=QT[:, :], i=qtd: e.dma_start(out=o, in_=i), inc="LQ", dma=True)
    p.run("sync", lambda e, o=Vx[:, :], i=vxd: e.dma_start(out=o, in_=i), inc="LQ", dma=True)
    p.run("sync", lambda e, o=cneg_sb, i=cnegd: e.dma_start(out=o, in_=i), inc="LQ", dma=True)
    p.run("sync", lambda e, o=rv_f[:, :], i=rv: e.dma_start(out=o, in_=i), inc="LQ", dma=True)
    p.run("sync", lambda e, o=t64[:, :], i=rv[64:65, :]: e.dma_start(out=o, in_=i),
          inc="LQ", dma=True)
    p.run("gpsimd", lambda e, o=iden_sb[:, :], i=iden: e.dma_start(out=o, in_=i),
          inc="LG", dma=True)
    p.run("gpsimd", lambda e, o=rk_sb[0:64, :], i=rkT: e.dma_start(out=o, in_=i),
          inc="LG", dma=True)
    lg = p.run("gpsimd", lambda e, o=rk_sb[64:128, :], i=rkT: e.dma_start(out=o, in_=i),
               inc="LG", dma=True)
    lq = p.run("sync", lambda e, o=bo_sb[:, :], i=bo: e.dma_start(out=o, in_=i),
               inc="LQ", dma=True)
    p.run("vector", lambda e, o=onesf2[:, :]: e.memset(o, 1.0), inc="V")
    p.run("vector", lambda e, o=ones127[:, :]: e.memset(o, 1.0), inc="V")
    p.run("vector", lambda e, o=zvec[:, :]: e.memset(o, 0.0), inc="V")
    vz = p.run("vector", lambda e, o=zrow[:, :]: e.memset(o, 0.0), inc="V")

    p.wait("tensor", "LQ", lq)
    p.wait("tensor", "LG", lg)
    p.bank_wait("tensor", 0)
    t1 = p.run("tensor", lambda e, o=p.bank(0)[:, 0:1], l=c["ones_row_f"], r=cneg_sb:
               e.matmul(o, l, r, start=True, stop=True), inc="T")
    p.wait("scalar", "T", t1)
    s1 = p.run("scalar", lambda e, o=cbias, i=p.bank(0)[:, 0:1]:
               e.activation(o, i, AF.Copy), inc="S")
    p.bank_consumed(0, "S", s1)
    p.bank_wait("tensor", 1)
    t2 = p.run("tensor", lambda e, o=p.bank(1)[0:65, 0:DK0], l=c["ones_row_f"][:, 0:65],
               r=rv_f[0:1, :]: e.matmul(o, l, r, start=True, stop=True), inc="T")
    p.wait("vector", "T", t2)
    v1 = p.run("vector", lambda e, o=embXb[:, :], a=rv_f[:, :], b=p.bank(1)[0:65, 0:DK0]:
               e.tensor_sub(o, a, b), inc="V")
    p.bank_consumed(1, "V", v1)
    s2 = p.run("scalar", lambda e, o=r01[:, 0:DK0], i=rv_f[0:1, :]:
               e.activation(o, i, AF.Copy), inc="S")
    p.wait("vector", "LQ", lq)
    v2 = p.run("vector", lambda e, o=r01[:, DK0:2 * DK0], a=t64, b=rv_f[0:1, :]:
               e.tensor_sub(o, a, b), inc="V")
    p.wait("sync", "S", s2)
    p.wait("sync", "V", v2)
    g1 = p.run("sync", lambda e, o=bass.AP(e2d.tensor, 0, [[128, 1], [1, 128]]),
               i=r01[:, :]: e.dma_start(out=o, in_=i), inc="EQ", dma=True)
    p.wait("sync", "EQ", g1)
    g2 = p.run("sync", lambda e, o=e_dw66[64:66, :], i=e2d: e.dma_start(out=o, in_=i),
               inc="EQ", dma=True)

    ptd2 = nc.dram_tensor("ptd2", [2 * NH0, T], BF16, kind="Internal").ap()
    p.run("vector", lambda e, o=xst1[:, :]: e.memset(o, 0.0), inc="V")
    p.run("vector", lambda e, o=xst2[:, :]: e.memset(o, 0.0), inc="V")
    zd = nc.dram_tensor("zd", [1, 319], BF16, kind="Internal").ap()
    p.wait("sync", "V", vz)
    gzd = p.run("sync", lambda e, o=zd, i=zrow: e.dma_start(out=o, in_=i),
                inc="PPQ", dma=True)
    p.wait("sync", "PPQ", gzd)
    p.run("sync", lambda e,
          o=bass.AP(ppad.tensor, 0, [[319, 32], [1, 319]]),
          i=bass.AP(zd.tensor, 0, [[0, 32], [1, 319]]),
          : e.dma_start(out=o, in_=i), inc="PPQ", dma=True)
    p.run("sync", lambda e,
          o=bass.AP(ppad.tensor, (PHROW - 32) * 319, [[PHROW * 319, 16], [319, 64], [1, 319]]),
          i=bass.AP(zd.tensor, 0, [[0, 16], [0, 64], [1, 319]]),
          : e.dma_start(out=o, in_=i), inc="PPQ", dma=True)

    p.run("sync", lambda e,
          o=bass.AP(wpad.tensor, 0, [[(PHROW + 32) * 319, 2], [319, 32], [1, 319]]),
          i=bass.AP(zd.tensor, 0, [[0, 2], [0, 32], [1, 319]]),
          : e.dma_start(out=o, in_=i), inc="PPQ", dma=True)

    for h in range(NH0):
        kt, ro = h // 2, (h % 2) * 64
        for it in range(16):
            pb = it % 2
            p.bank_wait("tensor", pb)
            tc = p.run("tensor", lambda e, o=p.bank(pb)[0:128, 0:65],
                       l=QT[ro:ro + 64, kt * T + it * 128:kt * T + (it + 1) * 128],
                       r=rk_sb[ro:ro + 64, :]:
                       e.matmul(o, l, r, start=True, stop=True), inc="T")
            p.wait("scalar", "T", tc)
            p.wait("scalar", "V", p.cnt["V"])
            sc = p.run("scalar", lambda e, o=ppst_v[:, it * 319 + 127:it * 319 + 192],
                       i=p.bank(pb)[0:128, 0:65]:
                       e.activation(o, i, AF.Copy), inc="S")
            p.bank_consumed(pb, "S", sc)
            p.wait("vector", "S", sc)
            p.run("vector", lambda e, o=ppst_v[:, it * 319:it * 319 + 127],
                  i=ones127, s=p.bank(pb)[0:128, 0:1]:
                  e.tensor_scalar_mul(o, i, s), inc="V")
            vpp = p.run("vector", lambda e, o=ppst_v[:, it * 319 + 192:it * 319 + 319],
                        i=ones127, s=p.bank(pb)[0:128, 64:65]:
                        e.tensor_scalar_mul(o, i, s), inc="V")
            p.bank_consumed(pb, "V", vpp)
        vdone = p.cnt["V"]
        p.wait("sync", "V", vdone)
        p.wait("sync", "S", p.cnt["S"])
        p.wait("sync", "PPQ", p.cnt["PPQ"])
        base = 32 + h * PHROW
        p.run("sync", lambda e,
              o=bass.AP(ppad.tensor, base * 319, [[319, 128], [128 * 319, 16], [1, 319]]),
              i=ppst_v.rearrange("q (t z) -> q t z", t=16),
              : e.dma_start(out=o, in_=i), inc="PPQ", dma=True)
        for which, col in ((0, 0), (1, 64)):
            for n in range(4):
                nsl = slice(n * 512, (n + 1) * 512)
                pb = 2 + n % 2
                p.bank_wait("tensor", pb)
                tc = p.run("tensor", lambda e, o=p.bank(pb)[0:1, :],
                           l=rk_sb[ro:ro + 64, col:col + 1],
                           r=QT[ro:ro + 64, kt * T:(kt + 1) * T][:, nsl]:
                           e.matmul(o, l, r, start=True, stop=True), inc="T")
                p.wait("scalar", "T", tc)
                sc = p.run("scalar", lambda e, o=pt0t[:, nsl] if which == 0 else pt64t[:, nsl],
                           i=p.bank(pb)[0:1, :]:
                           e.activation(o, i, AF.Copy), inc="S")
                p.bank_consumed(pb, "S", sc)
        p.wait("sync", "S", p.cnt["S"])
        p.run("sync", lambda e, o=ptd2[2 * h:2 * h + 1, :], i=pt0t[:, :]:
              e.dma_start(out=o, in_=i), inc="PPQ", dma=True)
        gpt = p.run("sync", lambda e, o=ptd2[2 * h + 1:2 * h + 2, :], i=pt64t[:, :]:
                    e.dma_start(out=o, in_=i), inc="PPQ", dma=True)
        p.wait("scalar", "PPQ", gpt)
    ppq_all = p.cnt["PPQ"]

    import os as _os
    nh_run = int(_os.environ.get("NH_RUN", str(NH0)))
    for h in range(nh_run):
        MILESTONES.append(("head", h, len(p.ops["tensor"]) + len(p.ops["scalar"]) + len(p.ops["vector"]) + len(p.ops["sync"]) + len(p.ops["gpsimd"])))
        kt, ro = h // 2, (h % 2) * 64
        pbase = 32 + h * PHROW
        wbase = 32

        p.wait("sync", "PPQ", ppq_all)
        p.wait("sync", "T", p.cnt["T"])
        p.run("sync", lambda e, o=KT_h[ro:ro + 64, :], i=ktd[ro:ro + 64, kt * T:(kt + 1) * T]:
              e.dma_start(out=o, in_=i), inc="BQ", dma=True)
        p.run("sync", lambda e,
              o=apc1.rearrange("q (t z) -> q t z", t=16),
              i=bass.AP(ppad.tensor, (pbase - 32) * 319 + 191,
                        [[318, 128], [128 * 319, 16], [1, 128]]),
              : e.dma_start(out=o, in_=i), inc="BQ", dma=True)
        p.run("sync", lambda e,
              o=apc2.rearrange("q (t z) -> q t z", t=16),
              i=bass.AP(ppad.tensor, (pbase + 96) * 319 + 63,
                        [[318, 64], [128 * 319, 16], [1, 128]]),
              : e.dma_start(out=o, in_=i), inc="BQ", dma=True)
        p.run("sync", lambda e, o=pt0t[:, :], i=ptd2[2 * h:2 * h + 1, :]:
              e.dma_start(out=o, in_=i), inc="BQ", dma=True)
        gb = p.run("sync", lambda e, o=pt64t[:, :], i=ptd2[2 * h + 1:2 * h + 2, :]:
                   e.dma_start(out=o, in_=i), inc="BQ", dma=True)
        p.wait("sync", "WXQ", p.cnt.get("WXQ", 0))
        p.wait("sync", "V", vz)
        gz = p.run("sync", lambda e,
                   o=bass.AP(wpad.tensor, wbase * 319 + 127, [[319, T], [1, 192]]),
                   i=bass.AP(zd.tensor, 0, [[0, T], [1, 192]]),
                   : e.dma_start(out=o, in_=i), inc="WZQ", dma=True)

        av_first = [True] * 4
        exp_s = [0, 0]
        for jt in range(16):
            j0 = jt * 128
            w0g, w1g = max(0, j0 - 32), min(T, j0 + 160)
            eb = jt % 2
            p.bank_wait("tensor", 6)
            p.wait("tensor", "BQ", gb)
            p.wait("tensor", "LQ", lq)
            p.wait("tensor", "LG", lg)
            p.run("tensor", lambda e, o=p.bank(6)[:, 0:128],
                  l=apc1[:, jt * 128:(jt + 1) * 128], r=iden_sb[:, :]:
                  e.matmul(o, l, r, start=True, stop=True), inc=None)
            tband = p.run("tensor", lambda e, o=p.bank(6)[:, 128:192],
                          l=apc2[:, jt * 128:(jt + 1) * 128], r=iden_sb[0:64, 0:64]:
                          e.matmul(o, l, r, start=True, stop=True), inc="T")
            p.wait("vector", "T", tband)
            p.wait("vector", "V", p.cnt["V"])
            vb = p.run("vector", lambda e, o=bnd_sb[:, :], i=p.bank(6)[:, 0:192]:
                       e.tensor_copy(o, i), inc="V")
            p.bank_consumed(6, "V", vb)

            for n in range(4):
                n0, n1 = n * 512, (n + 1) * 512
                pbk = n % 2 + 4
                p.bank_wait("tensor", pbk)
                rL = (n0, min(n1, max(n0, j0 - 32)))
                rR = (max(n0, min(n1, j0 + 160)), n1)
                hasL = rL[1] > rL[0]
                hasR = rR[1] > rR[0]
                tqk = p.run("tensor", lambda e, o=p.bank(pbk),
                            l=KT_h[ro:ro + 64, j0:j0 + 128],
                            r=QT[ro:ro + 64, kt * T + n0:kt * T + n1],
                            sp=not (hasL or hasR):
                            e.matmul(o, l, r, start=True, stop=sp),
                            inc=None if (hasL or hasR) else "T")

                def clip_mm(rng, row_ap, sp):
                    a, b = rng
                    return p.run("tensor", lambda e, o=p.bank(pbk)[:, a - n0:b - n0],
                                 l=c["ones_row_bf"], r=row_ap[:, a:b], sp=sp:
                                 e.matmul(o, l, r, start=False, stop=sp),
                                 inc="T" if sp else None)

                if hasL:
                    tqk = clip_mm(rL, pt64t, not hasR)
                if hasR:
                    tqk = clip_mm(rR, pt0t, True)
                a = max(w0g, n0)
                b = min(w1g, n1)
                if a < b:
                    p.wait("vector", "T", tqk)
                    p.wait("vector", "V", vb)
                    p.run("vector", lambda e, o=p.bank(pbk)[:, a - n0:b - n0],
                          x1=p.bank(pbk)[:, a - n0:b - n0],
                          x2=bnd_sb[:, (a - (j0 - 32)):(b - (j0 - 32))]:
                          e.tensor_add(o, x1, x2), inc="V")
                vband = p.cnt["V"]
                p.wait("scalar", "T", tqk)
                p.wait("scalar", "V", vband)
                sxp = p.run("scalar", lambda e, o=expT[eb][:, n0:n1], i=p.bank(pbk),
                            bi=cbias: e.activation(o, i, AF.Exp, bias=bi), inc="S")
                p.bank_consumed(pbk, "S", sxp)
                exp_s[eb] = sxp
                col = jt * NH0 * VEC + h * VEC
                cpb = n % 4

                if av_first[n]:
                    av_first[n] = False
                    p.bank_wait("tensor", cpb)
                    p.wait("tensor", "S", sxp)
                    p.run("tensor", lambda e, o=p.bank(cpb)[0:66, :],
                          l=zvec[:, :], r=expT[eb][:, n0:n1]:
                          e.matmul(o, l, r, start=True, stop=False), inc=None)

                def av_mm(a2, b2, ncols, sp=False):
                    if a2 >= b2:
                        return None
                    p.wait("tensor", "S", sxp)
                    return p.run("tensor",
                          lambda e, o=p.bank(cpb)[0:ncols, a2 - n0:b2 - n0],
                          l=Vx[:, col:col + ncols], r=expT[eb][:, a2:b2], sp=sp:
                          e.matmul(o, l, r, start=False, stop=sp),
                          inc="T" if sp else None)

                av_mm(rL[0], rL[1], 66)
                av_mm(max(w0g, n0), min(w1g, n1), 65)
                av_mm(rR[0], rR[1], 65)

            p.bank_wait("tensor", 7)
            i1a = max(0, j0 - 32)
            wid1 = min(T, j0 + 96) - i1a
            wid2 = min(T, j0 + 160) - (j0 + 96)
            p.wait("tensor", "S", exp_s[eb])
            p.run("tensor", lambda e, o=p.bank(7)[0:wid1, 0:128],
                  l=expT[eb][:, i1a:i1a + wid1], r=iden_sb[:, :]:
                  e.matmul(o, l, r, start=True, stop=True), inc=None)
            tx = p.run("tensor", lambda e, o=p.bank(7)[0:wid2, 128:256],
                       l=expT[eb][:, j0 + 96:j0 + 96 + wid2], r=iden_sb[:, :]:
                       e.matmul(o, l, r, start=True, stop=True), inc="T")
            p.wait("vector", "T", tx)
            p.wait("vector", "WXQ", p.cnt.get("WXQ", 0))
            p.run("vector", lambda e, o=xst1[:, jt * 128:(jt + 1) * 128][0:wid1, :],
                  i=p.bank(7)[0:wid1, 0:128]: e.tensor_copy(o, i), inc="V")
            p.run("vector", lambda e, o=xst2[:, jt * 128:(jt + 1) * 128][0:wid2, :],
                  i=p.bank(7)[0:wid2, 128:256]: e.tensor_copy(o, i), inc="V")
            vx_ = p.cnt["V"]
            p.bank_consumed(7, "V", vx_)
        tav = None
        for n in range(4):
            tav = p.run("tensor", lambda e, o=p.bank(n)[0:66, :], l=zvec[:, :],
                        r=expT[0][:, n * 512:(n + 1) * 512]:
                        e.matmul(o, l, r, start=False, stop=True), inc="T")

        MILESTONES.append(("dw", h, len(p.ops["tensor"]) + len(p.ops["scalar"]) + len(p.ops["vector"]) + len(p.ops["sync"]) + len(p.ops["gpsimd"])))
        for n in range(4):
            nsl = slice(n * 512, (n + 1) * 512)
            p.wait("scalar", "T", tav)
            p.run("scalar", lambda e, o=dwf[64:66, nsl], i=p.bank(n)[64:66, :]:
                  e.activation(o, i, AF.Copy), inc="S")
        p.wait("vector", "S", p.cnt["S"])

        def _recip(e, o=rdenf[64:65, :], i=dwf[64:65, :]):
            with e.bass.allow_low_precision("1/den stored bf16"):
                return e.reciprocal(o, i)

        vr = p.run("vector", lambda e: _recip(e), inc="V")

        p.wait("sync", "V", p.cnt["V"])
        p.wait("sync", "WZQ", p.cnt["WZQ"])
        p.run("sync", lambda e,
              o=bass.AP(wpad.tensor, wbase * 319 + 159, [[318, 96], [1, 128]]),
              i=xst1[0:96, 0:128],
              : e.dma_start(out=o, in_=i), inc="WXQ", dma=True)
        p.run("sync", lambda e,
              o=bass.AP(wpad.tensor, (wbase + 96) * 319 + 191,
                        [[318, 128], [128 * 319, 15], [1, 128]]),
              i=xst1.rearrange("q (t z) -> q t z", t=16)[:, 1:16, :],
              : e.dma_start(out=o, in_=i), inc="WXQ", dma=True)
        gx = p.run("sync", lambda e,
                   o=bass.AP(wpad.tensor, (wbase + 96) * 319 + 63,
                             [[318, 64], [128 * 319, 16], [1, 128]]),
                   i=xst2.rearrange("q (t z) -> q t z", t=16),
                   : e.dma_start(out=o, in_=i), inc="WXQ", dma=True)
        p.wait("sync", "WXQ", gx)
        gx = p.run("sync", lambda e,
                   o=wim.rearrange("q (t z) -> q t z", t=16),
                   i=bass.AP(wpad.tensor, wbase * 319 + 127,
                             [[319, 128], [128 * 319, 16], [1, 65]]),
                   : e.dma_start(out=o, in_=i), inc="WXQ", dma=True)

        p.wait("tensor", "WXQ", gx)
        for it in range(16):
            pb = 6 if it % 2 == 0 else 7
            p.bank_wait("tensor", pb)
            tw = p.run("tensor", lambda e, o=p.bank(pb)[0:65, 0:128],
                       l=wim[:, it * 65:(it + 1) * 65], r=iden_sb[:, :]:
                       e.matmul(o, l, r, start=True, stop=True), inc="T")
            p.wait("scalar", "T", tw)
            sw = p.run("scalar", lambda e, o=WunX[:, it * 128:(it + 1) * 128],
                       i=p.bank(pb)[0:65, 0:128]: e.activation(o, i, AF.Copy), inc="S")
            p.bank_consumed(pb, "S", sw)

        p.wait("tensor", "EQ", p.cnt["EQ"])
        p.wait("tensor", "S", p.cnt["S"])
        for n in range(4):
            nsl = slice(n * 512, (n + 1) * 512)
            prel = 4 + n % 2
            p.bank_wait("tensor", prel)
            p.run("tensor", lambda e, o=p.bank(prel)[0:64, :], l=embXb[:, :],
                  r=WunX[:, nsl]: e.matmul(o, l, r, start=True, stop=False), inc=None)
            trel = p.run("tensor", lambda e, o=p.bank(prel)[0:64, :], l=e_dw66[64:66, :],
                         r=dwf[64:66, nsl]: e.matmul(o, l, r, start=False, stop=True), inc="T")
            pbb = 6 + n % 2
            p.bank_wait("tensor", pbb)
            p.wait("tensor", "V", vr)
            tb = p.run("tensor", lambda e, o=p.bank(pbb), l=onesf2[64:65, :],
                       r=rdenf[64:65, nsl]:
                       e.matmul(o, l, r, start=True, stop=True), inc="T")
            p.wait("scalar", "T", tav)
            p.wait("scalar", "V", p.cnt["V"])
            su = p.run("scalar", lambda e, o=ctxu_sb, i=p.bank(n)[0:64, :]:
                       e.activation(o, i, AF.Copy), inc="S")
            p.bank_consumed(n, "S", su)
            p.wait("scalar", "T", tb)
            sb_ = p.run("scalar", lambda e, o=rdenb, i=p.bank(pbb):
                        e.activation(o, i, AF.Copy), inc="S")
            p.bank_consumed(pbb, "S", sb_)
            p.wait("vector", "S", sb_)
            p.wait("vector", "T", trel)
            va = p.run("vector", lambda e, o=ctxu_sb, a=ctxu_sb, b=p.bank(prel)[0:64, :]:
                       e.tensor_add(o, a, b), inc="V")
            p.bank_consumed(prel, "V", va)
            p.wait("vector", "V", va)
            vc = p.run("vector", lambda e, o=ctx[ro:ro + 64, kt * T:(kt + 1) * T][:, nsl],
                       a=ctxu_sb, b=rdenb[0:64, :]:
                       e.tensor_mul(o, a, b), inc="V")
            p.bank_consumed(n, "V", vc) if False else None
    ctx_v = p.cnt["V"]

    strip_state = {"free": [("T", 0), ("T", 0)]}
    for m in range(NKT):
        sb_i = m % 2
        s, cv = strip_state["free"][sb_i]
        p.wait("gpsimd", s, cv)
        srcw = WoT[:, m * 128:(m + 1) * 128].rearrange("(k q) j -> q k j", q=128)
        dstw = strips[sb_i].rearrange("q (k j) -> q k j", k=NKT)
        g = p.run("gpsimd", lambda e, o=dstw, i=srcw: e.dma_start(out=o, in_=i),
                  inc=f"WQ{sb_i}", dma=True)
        p.wait("tensor", f"WQ{sb_i}", g)
        p.wait("tensor", "V", ctx_v)
        tc = None
        for n in range(4):
            nsl = slice(n * 512, (n + 1) * 512)
            pb = 2 * (n % 2)
            p.bank_wait("tensor", pb)
            for k in range(NKT):
                tc = p.run("tensor", lambda e, o=p.bank(pb),
                      l=strips[sb_i][:, k * 128:(k + 1) * 128],
                      r=ctx[:, k * T:(k + 1) * T][:, nsl], st=(k == 0), sp=(k == NKT - 1):
                      e.matmul(o, l, r, start=st, stop=sp),
                      inc="T" if k == NKT - 1 else None)
            p.wait("vector", "T", tc)
            ssc = p.run("vector", lambda e, o=ybuf[:, m * T:(m + 1) * T][:, nsl],
                        i=p.bank(pb), b=bo_sb[:, m:m + 1]:
                        e.tensor_scalar_add(o, i, b), inc="V")
            p.bank_consumed(pb, "V", ssc)
        strip_state["free"][sb_i] = ("T", tc)
    y_s = p.cnt["V"]

    xs = XStream(p, "xld", xin, nbuf=1)

    def src_y(k):
        return ybuf[:, k * T:(k + 1) * T], "V", y_s

    def apply_post(k, n, yt, ysname, ycnt, pa, pb_):
        sl = slice(n * 512, (n + 1) * 512)
        if n % 2 == 0:
            p.wait("sync", "V", p.cnt["V"])
            p.wait("sync", "S", p.cnt["S"])
            p.wait("sync", "T", p.cnt["T"])
            ap, sname, g, b = xs.load(k)
            apply_post.res = (ap, sname, g)
        rap, rs, rc = apply_post.res
        ob = outb[0]
        p.wait("vector", ysname, ycnt)
        p.wait("vector", "V", apply_post.t_free)
        v1 = p.run("vector", lambda e, o=ln.t1, i=yt[:, sl], b=pa:
                   e.tensor_mul(o, i, b), inc="V")
        p.wait("vector", "V", v1)
        v2 = p.run("vector", lambda e, o=ln.t2, i=ln.t1, b=pb_:
                   e.tensor_sub(o, i, b), inc="V")
        p.wait("vector", rs, rc)
        p.wait("vector", "OQ0", apply_post.ob_free.get(0, 0))
        p.wait("vector", "V", v2)
        v3 = p.run("vector", lambda e, o=ob[:, :], a=ln.t2, b=rap[:, sl]:
                   e.tensor_add(o, a, b), inc="V")
        apply_post.t_free = v3
        p.wait("sync", "V", v3)
        dcnt = p.run("sync", lambda e, o=xout[k * 128:(k + 1) * 128, sl], i=ob[:, :]:
                     e.dma_start(out=o, in_=i), inc="OQ0", dma=True)
        apply_post.ob_free[0] = dcnt

    apply_post.t_free = 0
    apply_post.ob_free = {}
    apply_post.res = None

    emit_ln(p, c, ln, src_y, apply_post, src_bf16=True)
    p.wait("sync", "OQ0", p.cnt.get("OQ0", 0))


def attn0_in_maps(x_bt, W):
    com1 = {
        "WqT": _t(W["Wq"]), "WkT": _t(W["Wk"]), "WvT": _t(W["Wv"]),
        "rkT": _t(W["rel_k_emb"]),
    }
    com2 = {
        "rkT": _t(W["rel_k_emb"]),
        "rv": np.ascontiguousarray(np.asarray(W["rel_v_emb"], np.float32)),
        "WoT": _t(W["Wo"]),
        "bo": _t(np.asarray(W["b_o"], np.float32).reshape(NKT, 128)),
        "iden": np.eye(128, dtype=np.float32),
    }
    m1 = [dict(com1, xin=_t(x_bt[b])) for b in range(x_bt.shape[0])]
    m2 = [dict(com2, xin=_t(x_bt[b])) for b in range(x_bt.shape[0])]
    return m1, m2





DHID1 = 3072
NM1 = DHID1 // 128


def build_ffn1(p):
    nc = p.nc
    xin = nc.dram_tensor("xin", [D, T], F32, kind="ExternalInput").ap()
    wgT = nc.dram_tensor("wgT", [D, DHID1], I32, kind="ExternalInput").ap()
    wuT = nc.dram_tensor("wuT", [D, DHID1], I32, kind="ExternalInput").ap()
    wdT = nc.dram_tensor("wdT", [DHID1, D], I32, kind="ExternalInput").ap()
    xout = nc.dram_tensor("xout", [D, T], F32, kind="ExternalOutput").ap()

    c = make_common(p)
    ln = LNBufs(p, "ln")

    xs = XStream(p, "xld", xin)
    xln = p.sb("xln", [128, NKT * T], BF16)
    ybuf = xln
    hbuf = p.sb("hbuf", [128, NM1 * T], BF16)
    strips = [p.sb(f"strip{i}", [128, 2 * NKT * 128], BF16) for i in range(2)]
    dstrips = [p.sb(f"dstrip{i}", [128, NM1 * 128], BF16) for i in range(2)]
    tsil = p.sb("tsil", [128, 512], BF16)
    outb = [p.sb(f"outb{i}", [128, 512], F32) for i in range(2)]

    def src2(k):
        p.wait("sync", "S", p.cnt.get("S", 0))
        p.wait("sync", "V", p.cnt.get("V", 0))
        p.wait("sync", "T", p.cnt.get("T", 0))
        ap, sname, g, b = xs.load(k)
        return ap, sname, g

    def apply_pre(k, n, xt, xsname, xcnt, pa, pb_):
        sl = slice(n * 512, (n + 1) * 512)
        p.wait("vector", xsname, xcnt)
        p.wait("vector", "V", apply_pre.t1_free)
        v1 = p.run("vector", lambda e, o=ln.t1, i=xt[:, sl], b=pa:
                   e.tensor_mul(o, i, b), inc="V")
        p.wait("vector", "V", v1)
        v2 = p.run("vector", lambda e, o=xln[:, k * T:(k + 1) * T][:, sl], i=ln.t1, b=pb_:
                   e.tensor_sub(o, i, b), inc="V")
        apply_pre.t1_free = v2

    apply_pre.t1_free = 0
    emit_ln(p, c, ln, src2, apply_pre)
    xln_v = p.cnt["V"]

    strip_state = {"free": [("T", 0), ("T", 0)]}

    for m in range(NM1):
        sb_i = m % 2
        s, cv = strip_state["free"][sb_i]
        p.wait("gpsimd", s, cv)
        src1 = wgT[:, m * 128:(m + 1) * 128].rearrange("(k q) j -> q k j", q=128)
        src2_ = wuT[:, m * 128:(m + 1) * 128].rearrange("(k q) j -> q k j", q=128)
        dst1 = strips[sb_i][:, 0:NKT * 128].rearrange("q (k j) -> q k j", k=NKT)
        dst2 = strips[sb_i][:, NKT * 128:2 * NKT * 128].rearrange("q (k j) -> q k j", k=NKT)
        p.run("gpsimd", lambda e, o=dst1, i=src1: e.dma_start(out=o, in_=i),
              inc=f"WQ{sb_i}", dma=True)
        g = p.run("gpsimd", lambda e, o=dst2, i=src2_: e.dma_start(out=o, in_=i),
              inc=f"WQ{sb_i}", dma=True)
        p.wait("tensor", f"WQ{sb_i}", g)
        p.wait("tensor", "V", xln_v)
        tc = None
        for n in range(4):
            nsl = slice(n * 512, (n + 1) * 512)
            pb1, pb2 = 2 * (n % 2), 2 * (n % 2) + 1
            p.bank_wait("tensor", pb1)
            p.bank_wait("tensor", pb2)
            for k in range(NKT):
                p.run("tensor", lambda e, o=p.bank(pb1), l=strips[sb_i][:, k * 128:(k + 1) * 128],
                      r=xln[:, k * T:(k + 1) * T][:, nsl], st=(k == 0), sp=(k == NKT - 1):
                      e.matmul(o, l, r, start=st, stop=sp), inc=None)
            for k in range(NKT):
                tc = p.run("tensor", lambda e, o=p.bank(pb2),
                      l=strips[sb_i][:, NKT * 128 + k * 128:NKT * 128 + (k + 1) * 128],
                      r=xln[:, k * T:(k + 1) * T][:, nsl], st=(k == 0), sp=(k == NKT - 1):
                      e.matmul(o, l, r, start=st, stop=sp),
                      inc="T" if k == NKT - 1 else None)
            p.wait("scalar", "T", tc)
            p.wait("scalar", "V", p.cnt["V"])
            ssc = p.run("scalar", lambda e, o=tsil, i=p.bank(pb1), s=c["cc"]("inv127"):
                        e.activation(o, i, AF.Silu, scale=s), inc="S")
            p.wait("vector", "S", ssc)
            vc = p.run("vector", lambda e, o=hbuf[:, m * T:(m + 1) * T][:, nsl],
                       i=p.bank(pb2), t=tsil:
                       e.scalar_tensor_tensor(o, i, 1.0, t, ALU.mult, ALU.mult), inc="V")
            p.bank_consumed(pb1, "S", ssc)
            p.bank_consumed(pb2, "V", vc)
        strip_state["free"][sb_i] = ("T", tc)
    h_v = p.cnt["V"]

    for m in range(NKT):
        sb_i = m % 2
        s, cv = strip_state["free"][sb_i]
        p.wait("gpsimd", s, cv)
        srcw = wdT[:, m * 128:(m + 1) * 128].rearrange("(k q) j -> q k j", q=128)
        dstw = dstrips[sb_i].rearrange("q (k j) -> q k j", k=NM1)
        g = p.run("gpsimd", lambda e, o=dstw, i=srcw: e.dma_start(out=o, in_=i),
                  inc=f"WQ{sb_i}", dma=True)
        p.wait("tensor", f"WQ{sb_i}", g)
        p.wait("tensor", "V", h_v)
        tc = None
        for n in range(4):
            nsl = slice(n * 512, (n + 1) * 512)
            pb = 4 + n % 2
            p.bank_wait("tensor", pb)
            for k in range(NM1):
                tc = p.run("tensor", lambda e, o=p.bank(pb),
                      l=dstrips[sb_i][:, k * 128:(k + 1) * 128],
                      r=hbuf[:, k * T:(k + 1) * T][:, nsl], st=(k == 0), sp=(k == NM1 - 1):
                      e.matmul(o, l, r, start=st, stop=sp),
                      inc="T" if k == NM1 - 1 else None)
            p.wait("scalar", "T", tc)
            ssc = p.run("scalar", lambda e, o=ybuf[:, m * T:(m + 1) * T][:, nsl],
                        i=p.bank(pb), s=c["cc"]("inv127sq"):
                        e.activation(o, i, AF.Copy, scale=s), inc="S")
            p.bank_consumed(pb, "S", ssc)
        strip_state["free"][sb_i] = ("T", tc)
    y_s = p.cnt["S"]

    def src_y(k):
        return ybuf[:, k * T:(k + 1) * T], "S", y_s

    xs2 = xs

    def apply_post(k, n, yt, ysname, ycnt, pa, pb_):
        sl = slice(n * 512, (n + 1) * 512)
        if n % 2 == 0:
            p.wait("sync", "V", p.cnt["V"])
            p.wait("sync", "S", p.cnt["S"])
            p.wait("sync", "T", p.cnt["T"])
            ap, sname, g, b = xs2.load(k)
            apply_post.res = (ap, sname, g)
        rap, rs, rc = apply_post.res
        ob = outb[0]
        p.wait("vector", ysname, ycnt)
        p.wait("vector", "V", apply_post.t_free)
        v1 = p.run("vector", lambda e, o=ln.t1, i=yt[:, sl], b=pa:
                   e.tensor_mul(o, i, b), inc="V")
        p.wait("vector", "V", v1)
        v2 = p.run("vector", lambda e, o=ln.t2, i=ln.t1, b=pb_:
                   e.tensor_sub(o, i, b), inc="V")
        p.wait("vector", rs, rc)
        p.wait("vector", "OQ0", apply_post.ob_free.get(0, 0))
        p.wait("vector", "V", v2)
        v3 = p.run("vector", lambda e, o=ob[:, :], a=ln.t2, b=rap[:, sl]:
                   e.tensor_add(o, a, b), inc="V")
        apply_post.t_free = v3
        p.wait("sync", "V", v3)
        dcnt = p.run("sync", lambda e, o=xout[k * 128:(k + 1) * 128, sl], i=ob[:, :]:
                     e.dma_start(out=o, in_=i), inc="OQ0", dma=True)
        apply_post.ob_free[0] = dcnt

    apply_post.t_free = 0
    apply_post.ob_free = {}
    apply_post.res = None

    emit_ln(p, c, ln, src_y, apply_post, src_bf16=True)
    p.wait("sync", "OQ0", p.cnt.get("OQ0", 0))
    p.wait("sync", "OQ1", p.cnt.get("OQ1", 0))


def ffn1_in_maps(x_bt, W):
    com = {
        "wgT": _t(W["w_gate"]).astype(np.int32),
        "wuT": _t(W["w_up"]).astype(np.int32),
        "wdT": _t(W["w_down"]).astype(np.int32),
    }
    return [dict(com, xin=_t(x_bt[b])) for b in range(x_bt.shape[0])]



def _t(a):
    return np.ascontiguousarray(np.asarray(a).T)


def ffn0_in_maps(x_bt, W):
    sc = np.concatenate([
        _t(np.asarray(W["w1_s"]).reshape(NMH, 128)),
        _t(np.asarray(W["w2_s"]).reshape(NMH, 128)),
        _t(np.asarray(W["h1_s"]).reshape(NMH, 128)),
        _t(np.asarray(W["h2_s"]).reshape(NMH, 128)),
        _t(np.asarray(W["w3_s"]).reshape(NKT, 128)),
    ], axis=1)
    com = {
        "w1T": _t(W["w1_w"]), "w2T": _t(W["w2_w"]),
        "h1T": _t(W["h1_w"]), "h2T": _t(W["h2_w"]),
        "w3T": _t(W["w3_w"]), "sc": np.ascontiguousarray(sc),
    }
    return [dict(com, xin=_t(x_bt[b])) for b in range(x_bt.shape[0])]


def kernel(**inputs):
    x = np.asarray(inputs["L0_attn_x"], np.float32)
    nb = x.shape[0]
    W0 = {k[len("L0_attn_"):]: np.asarray(v) for k, v in inputs.items()
          if k.startswith("L0_attn_") and k != "L0_attn_x"}
    F0 = {k[len("L0_ffn_"):]: np.asarray(v) for k, v in inputs.items()
          if k.startswith("L0_ffn_")}
    A1 = {k[len("L1_attn_"):]: np.asarray(v) for k, v in inputs.items()
          if k.startswith("L1_attn_")}
    F1 = {k[len("L1_ffn_"):]: np.asarray(v) for k, v in inputs.items()
          if k.startswith("L1_ffn_")}

    m1, m2 = attn0_in_maps(x, W0)
    r = build_and_run(build_attn0_pre, m1, n_cores=nb)
    for b in range(nb):
        for key in ("qtd", "ktd", "vxd", "cnegd"):
            m2[b][key] = np.ascontiguousarray(r[b][key])
    r = build_and_run(build_attn0_core, m2, n_cores=nb)
    xT = [np.ascontiguousarray(r[b]["xout"]) for b in range(nb)]

    maps = ffn0_in_maps(x, F0)
    for b in range(nb):
        maps[b]["xin"] = xT[b]
    r = build_and_run(build_ffn0, maps, n_cores=nb)
    xT = [np.ascontiguousarray(r[b]["xout"]) for b in range(nb)]

    m1, m2, m3 = e88_in_maps(x, A1)
    for b in range(nb):
        m1[b]["xin"] = xT[b]
        m3[b]["xin"] = xT[b]
    r1 = build_and_run(build_e88_pre, m1, n_cores=nb)
    for b in range(nb):
        for key in ("kn_hnd", "qn_hnd", "kntmd", "vtmd", "dcold"):
            m2[b][key] = np.ascontiguousarray(r1[b][key])
    r2 = build_and_run(build_e88_scan, m2, n_cores=nb)
    for b in range(nb):
        m3[b]["sqd"] = np.ascontiguousarray(r2[b]["sqd"])
        m3[b]["gated"] = np.ascontiguousarray(r1[b]["gated"])
    r = build_and_run(build_e88_post, m3, n_cores=nb)
    xT = [np.ascontiguousarray(r[b]["xout"]) for b in range(nb)]

    maps = ffn1_in_maps(x, F1)
    for b in range(nb):
        maps[b]["xin"] = xT[b]
    r = build_and_run(build_ffn1, maps, n_cores=nb)
    out = np.stack([np.ascontiguousarray(r[b]["xout"].T) for b in range(nb)])
    return out.astype(np.float32)

H1 = 8
N1 = 32
HV1 = 64
KD1 = H1 * N1
VD1 = H1 * HV1
TP3 = T + 3


def build_e88_pre(p):
    nc = p.nc
    xin = nc.dram_tensor("xin", [D, T], F32, kind="ExternalInput").ap()
    wq1T = nc.dram_tensor("wq1T", [D, KD1], F32, kind="ExternalInput").ap()
    wk1T = nc.dram_tensor("wk1T", [D, KD1], F32, kind="ExternalInput").ap()
    wv1T = nc.dram_tensor("wv1T", [D, VD1], F32, kind="ExternalInput").ap()
    wg1T = nc.dram_tensor("wg1T", [D, VD1], F32, kind="ExternalInput").ap()
    wa1T = nc.dram_tensor("wa1T", [D, H1], F32, kind="ExternalInput").ap()
    alog = nc.dram_tensor("alog", [H1, 1], F32, kind="ExternalInput").ap()
    dtb = nc.dram_tensor("dtb", [H1, 1], F32, kind="ExternalInput").ap()
    qcw = nc.dram_tensor("qcw", [128, 2 * 4], F32, kind="ExternalInput").ap()
    kcw = nc.dram_tensor("kcw", [128, 2 * 4], F32, kind="ExternalInput").ap()
    vcw = nc.dram_tensor("vcw", [128, 4 * 4], F32, kind="ExternalInput").ap()
    blk4 = nc.dram_tensor("blk4", [128, 4], F32, kind="ExternalInput").ap()
    blk4T = nc.dram_tensor("blk4T", [4, 128], F32, kind="ExternalInput").ap()
    e8g = nc.dram_tensor("e8g", [H1, 2 * 128], F32, kind="ExternalInput").ap()
    iden = nc.dram_tensor("iden", [128, 128], F32, kind="ExternalInput").ap()
    kn_hnd = nc.dram_tensor("kn_hnd", [KD1, T], BF16, kind="ExternalOutput").ap()
    qn_hnd = nc.dram_tensor("qn_hnd", [KD1, T], BF16, kind="ExternalOutput").ap()
    kntmd = nc.dram_tensor("kntmd", [T, KD1], BF16, kind="ExternalOutput").ap()
    vtmd = nc.dram_tensor("vtmd", [T, VD1], BF16, kind="ExternalOutput").ap()
    dcold = nc.dram_tensor("dcold", [2 * 128, T], BF16, kind="ExternalOutput").ap()
    gated = nc.dram_tensor("gated", [VD1, T], BF16, kind="ExternalOutput").ap()

    c = make_common(p)
    ln = LNBufs(p, "ln")

    xs = XStream(p, "xld", xin)
    xln = p.sb("xln", [128, NKT * T], BF16)
    wq_sb = p.sb("wq_sb", [128, NKT * KD1], BF16)
    wk_sb = p.sb("wk_sb", [128, NKT * KD1], BF16)
    wv_sb = p.sb("wv_sb", [128, NKT * VD1], BF16)
    wg_sb = p.sb("wg_sb", [128, NKT * VD1], BF16)
    wa_sb = p.sb("wa_sb", [128, NKT * H1], BF16)
    cw_sb = p.sb("cw_sb", [128, 16], F32)
    vcw_sb = p.sb("vcw_sb", [128, 16], F32)
    al_sb = p.sb("al_sb", [H1, 2], F32)
    bk4 = p.sb("bk4", [128, 4], BF16)
    bk4T = p.sb("bk4T", [4, 128], BF16)
    e8_sb = p.sb("e8_sb", [H1, 2 * 128], BF16)
    iden_sb = p.sb("iden_sb", [128, 128], BF16)
    uraw = p.sb("uraw", [128, 4 * TP3], BF16)
    cva = p.sb("cva", [128, T], BF16)
    cvb = p.sb("cvb", [128, T], BF16)
    qc = p.sb("qc", [128, 2 * T], BF16)
    kc = p.sb("kc", [128, 2 * T], BF16)
    vc = p.sb("vc", [128, 4 * T], BF16)
    knb = p.sb("knb", [128, 2 * T], BF16)
    qnb = p.sb("qnb", [128, 2 * T], BF16)
    rnb = p.sb("rnb", [4, 512], F32)
    rnb2 = p.sb("rnb2", [4, 512], F32)
    bk4Tf = p.sb("bk4Tf", [4, 128], F32)
    araw = p.sb("araw", [H1, T], BF16)
    at1 = p.sb("at1", [H1, T], BF16)
    at2 = p.sb("at2", [H1, T], BF16)
    dec = p.sb("dec", [H1, T], BF16)
    eA = p.sb("eA", [H1, 1], F32)
    tmst = p.sb("tmst", [128, KD1 + VD1], BF16)

    def src2(k):
        p.wait("sync", "S", p.cnt.get("S", 0))
        p.wait("sync", "V", p.cnt.get("V", 0))
        p.wait("sync", "T", p.cnt.get("T", 0))
        ap, sname, g, b = xs.load(k)
        return ap, sname, g

    def apply_pre(k, n, xt, xsname, xcnt, pa, pb_):
        sl = slice(n * 512, (n + 1) * 512)
        p.wait("vector", xsname, xcnt)
        p.wait("vector", "V", apply_pre.t1_free)
        v1 = p.run("vector", lambda e, o=ln.t1, i=xt[:, sl], b=pa:
                   e.tensor_mul(o, i, b), inc="V")
        p.wait("vector", "V", v1)
        v2 = p.run("vector", lambda e, o=xln[:, k * T:(k + 1) * T][:, sl], i=ln.t1, b=pb_:
                   e.tensor_sub(o, i, b), inc="V")
        apply_pre.t1_free = v2

    apply_pre.t1_free = 0
    emit_ln(p, c, ln, src2, apply_pre)
    xln_v = p.cnt["V"]

    for dst, srcw in ((wq_sb, wq1T), (wk_sb, wk1T), (wv_sb, wv1T),
                      (wg_sb, wg1T), (wa_sb, wa1T)):
        p.run("gpsimd", lambda e, o=dst.rearrange("q (k j) -> q k j", k=NKT),
              i=srcw.rearrange("(k q) j -> q k j", q=128):
              e.dma_start(out=o, in_=i), inc="LG", dma=True)
    p.run("gpsimd", lambda e, o=bk4[:, :], i=blk4: e.dma_start(out=o, in_=i), inc="LG", dma=True)
    p.run("gpsimd", lambda e, o=bk4T[:, :], i=blk4T: e.dma_start(out=o, in_=i), inc="LG", dma=True)
    p.run("sync", lambda e, o=bk4Tf[:, :], i=blk4T: e.dma_start(out=o, in_=i), inc="LQ", dma=True)
    p.run("gpsimd", lambda e, o=e8_sb[:, :], i=e8g: e.dma_start(out=o, in_=i), inc="LG", dma=True)
    lg = p.run("gpsimd", lambda e, o=iden_sb[:, :], i=iden: e.dma_start(out=o, in_=i),
               inc="LG", dma=True)
    p.run("sync", lambda e, o=cw_sb[:, 0:8], i=qcw: e.dma_start(out=o, in_=i), inc="LQ", dma=True)
    p.run("sync", lambda e, o=cw_sb[:, 8:16], i=kcw: e.dma_start(out=o, in_=i), inc="LQ", dma=True)
    p.run("sync", lambda e, o=vcw_sb[:, :], i=vcw: e.dma_start(out=o, in_=i), inc="LQ", dma=True)
    p.run("sync", lambda e, o=al_sb[:, 0:1], i=alog: e.dma_start(out=o, in_=i), inc="LQ", dma=True)
    lq = p.run("sync", lambda e, o=al_sb[:, 1:2], i=dtb: e.dma_start(out=o, in_=i),
               inc="LQ", dma=True)

    vz = p.run("vector", lambda e,
               o=uraw.rearrange("q (k z) -> q k z", k=4)[:, :, 0:3]:
               e.memset(o, 0.0), inc="V")

    def proj_tile(wsb, wcols, m, mtile, put):
        out_s = 0
        for n in range(4):
            nsl = slice(n * 512, (n + 1) * 512)
            pb = n % 2
            p.bank_wait("tensor", pb)
            p.wait("tensor", "LG", lg)
            p.wait("tensor", "V", xln_v)
            tc = None
            for k in range(NKT):
                lslice = wsb[:, k * wcols + m * 128:k * wcols + m * 128 + mtile]
                tc = p.run("tensor", lambda e, o=p.bank(pb)[0:mtile, :], l=lslice,
                           r=xln[:, k * T:(k + 1) * T][:, nsl],
                           st=(k == 0), sp=(k == NKT - 1):
                           e.matmul(o, l, r, start=st, stop=sp),
                           inc="T" if k == NKT - 1 else None)
            p.wait("scalar", "T", tc)
            out_s = put(n, p.bank(pb)[0:mtile, :])
            p.bank_consumed(pb, "S", out_s)
        return out_s

    def conv_silu(kt, cwt, ccol0, dst):
        ub = uraw[:, kt * TP3:(kt + 1) * TP3]
        p.wait("vector", "S", p.cnt["S"])
        p.wait("vector", "LQ", lq)
        p.wait("vector", "V", p.cnt["V"])
        p.run("vector", lambda e, o=cva[:, :], i=ub[:, 0:T],
              s=cwt[:, ccol0:ccol0 + 1]:
              e.tensor_scalar_mul(o, i, s), inc="V")
        prev, cur = cva, cvb
        for tau in range(1, 4):
            p.wait("vector", "V", p.cnt["V"])
            p.run("vector", lambda e, o=cur[:, :], i=ub[:, tau:tau + T],
                  s=cwt[:, ccol0 + tau:ccol0 + tau + 1], a=prev[:, :]:
                  e.scalar_tensor_tensor(o, i, s, a, ALU.mult, ALU.add), inc="V")
            prev, cur = cur, prev
        p.wait("scalar", "V", p.cnt["V"])
        return p.run("scalar", lambda e, o=dst, i=prev[:, :]:
                     e.activation(o, i, AF.Silu), inc="S")

    for (wsb, wcols, ntile, cwt, ccolbase, cdst) in (
        (wq_sb, KD1, 2, cw_sb, 0, qc),
        (wk_sb, KD1, 2, cw_sb, 8, kc),
        (wv_sb, VD1, 4, vcw_sb, 0, vc),
    ):
        for m in range(ntile):
            ublock = m % 4
            ub_t = uraw[:, ublock * TP3 + 3:ublock * TP3 + 3 + T]

            def putu(n, ps, ub_t=ub_t):
                p.wait("scalar", "V", p.cnt["V"])
                return p.run("scalar", lambda e, o=ub_t[:, n * 512:(n + 1) * 512], i=ps:
                             e.activation(o, i, AF.Copy), inc="S")

            proj_tile(wsb, wcols, m, 128, putu)
            conv_silu(ublock, cwt, ccolbase + 4 * m, cdst[:, m * T:(m + 1) * T])

    gst = cva
    for m in range(4):
        def putg(n, ps):
            p.wait("scalar", "GQ", p.cnt.get("GQ", 0))
            return p.run("scalar", lambda e, o=gst[:, n * 512:(n + 1) * 512], i=ps:
                         e.activation(o, i, AF.Copy), inc="S")

        sdone = proj_tile(wg_sb, VD1, m, 128, putg)
        p.wait("sync", "S", sdone)
        p.wait("sync", "GQ", p.cnt.get("GQ", 0))
        p.run("sync", lambda e, o=gated[m * 128:(m + 1) * 128, :], i=gst[:, :]:
              e.dma_start(out=o, in_=i), inc="GQ", dma=True)

    def puta(n, ps):
        p.wait("vector", "S", p.cnt["S"])
        p.wait("vector", "LQ", lq)
        return_v = p.run("vector", lambda e, o=araw[:, n * 512:(n + 1) * 512], i=ps,
                         b=al_sb[:, 1:2]: e.tensor_scalar_add(o, i, b), inc="V")
        return return_v

    for n in range(4):
        nsl = slice(n * 512, (n + 1) * 512)
        pb = n % 2
        p.bank_wait("tensor", pb)
        p.wait("tensor", "LG", lg)
        p.wait("tensor", "V", xln_v)
        tc = None
        for k in range(NKT):
            tc = p.run("tensor", lambda e, o=p.bank(pb)[0:H1, :],
                       l=wa_sb[:, k * H1:(k + 1) * H1],
                       r=xln[:, k * T:(k + 1) * T][:, nsl],
                       st=(k == 0), sp=(k == NKT - 1):
                       e.matmul(o, l, r, start=st, stop=sp),
                       inc="T" if k == NKT - 1 else None)
        p.wait("vector", "T", tc)
        p.wait("vector", "LQ", lq)
        vv = p.run("vector", lambda e, o=araw[:, nsl], i=p.bank(pb)[0:H1, :],
                   b=al_sb[:, 1:2]: e.tensor_scalar_add(o, i, b), inc="V")
        p.bank_consumed(pb, "V", vv)
    p.wait("scalar", "V", p.cnt["V"])
    s1 = p.run("scalar", lambda e, o=at1[:, :], i=araw[:, :]:
               e.activation(o, i, AF.Abs), inc="S")
    p.wait("scalar", "S", s1)
    s2 = p.run("scalar", lambda e, o=at2[:, :], i=at1[:, :]:
               e.activation(o, i, AF.Exp, scale=-1.0), inc="S")
    p.wait("scalar", "S", s2)
    s3 = p.run("scalar", lambda e, o=at1[:, :], i=at2[:, :]:
               e.activation(o, i, AF.Ln, bias=1.0), inc="S")
    p.wait("vector", "S", s3)
    v1 = p.run("vector", lambda e, o=at2[:, :], i=araw[:, :]:
               e.tensor_scalar_max(o, i, 0.0), inc="V")
    p.wait("vector", "V", v1)
    v2 = p.run("vector", lambda e, o=at2[:, :], a=at2[:, :], b=at1[:, :]:
               e.tensor_add(o, a, b), inc="V")
    s4 = p.run("scalar", lambda e, o=eA[:, :], i=al_sb[:, 0:1]:
               e.activation(o, i, AF.Exp), inc="S")
    p.wait("vector", "S", s4)
    p.wait("vector", "V", v2)
    v3 = p.run("vector", lambda e, o=at1[:, :], i=at2[:, :], s=eA[:, 0:1]:
               e.tensor_scalar(o, i, s, -1.0, ALU.mult, ALU.mult), inc="V")
    p.wait("scalar", "V", v3)
    s5 = p.run("scalar", lambda e, o=dec[:, :], i=at1[:, :]:
               e.activation(o, i, AF.Exp), inc="S")

    dst_dc = cvb
    for g in range(2):
        for n in range(4):
            nsl = slice(n * 512, (n + 1) * 512)
            pb = 2 + n % 2
            p.bank_wait("tensor", pb)
            p.wait("tensor", "S", s5)
            tc = p.run("tensor", lambda e, o=p.bank(pb), l=e8_sb[:, g * 128:(g + 1) * 128],
                       r=dec[:, nsl]: e.matmul(o, l, r, start=True, stop=True), inc="T")
            p.wait("scalar", "T", tc)
            p.wait("scalar", "DCQ", p.cnt.get("DCQ", 0))
            sc = p.run("scalar", lambda e, o=dst_dc[:, nsl], i=p.bank(pb):
                       e.activation(o, i, AF.Copy), inc="S")
            p.bank_consumed(pb, "S", sc)
        p.wait("sync", "S", p.cnt["S"])
        p.run("sync", lambda e, o=dcold[g * 128:(g + 1) * 128, :], i=dst_dc[:, :]:
              e.dma_start(out=o, in_=i), inc="DCQ", dma=True)

    def normalize(srcb, dstb, ntile):
        for kt in range(ntile):
            for n in range(4):
                nsl = slice(n * 512, (n + 1) * 512)
                pb = 4 + n % 2
                sq = ln.x2[n % 2]
                p.wait("scalar", "T", p.cnt["T"])
                p.wait("scalar", "S", p.cnt["S"])
                ssq = p.run("scalar", lambda e, o=sq, i=srcb[:, kt * T:(kt + 1) * T][:, nsl]:
                            e.activation(o, i, AF.Square), inc="S")
                p.wait("tensor", "S", ssq)
                p.bank_wait("tensor", pb)
                tc = p.run("tensor", lambda e, o=p.bank(pb)[0:4, :], l=bk4, r=sq:
                           e.matmul(o, l, r, start=True, stop=True), inc="T")
                p.wait("scalar", "T", tc)
                sc = p.run("scalar", lambda e, o=rnb[:, :], i=p.bank(pb)[0:4, :]:
                           e.activation(o, i, AF.Sqrt), inc="S")
                p.bank_consumed(pb, "S", sc)
                p.wait("vector", "S", sc)
                v1 = p.run("vector", lambda e, o=rnb[:, :], i=rnb[:, :]:
                           e.tensor_scalar_add(o, i, 1e-6), inc="V")
                p.wait("vector", "V", v1)
                v2 = p.run("vector", lambda e, o=rnb2[:, :], i=rnb[:, :]:
                           e.reciprocal(o, i), inc="V")
                p.wait("tensor", "V", v2)
                p.bank_wait("tensor", 6 + n % 2)
                tb = p.run("tensor", lambda e, o=p.bank(6 + n % 2), l=bk4Tf,
                           r=rnb2[:, :]:
                           e.matmul(o, l, r, start=True, stop=True), inc="T")
                p.wait("vector", "T", tb)
                vm = p.run("vector", lambda e, o=dstb[:, kt * T:(kt + 1) * T][:, nsl],
                           a=srcb[:, kt * T:(kt + 1) * T][:, nsl], b=p.bank(6 + n % 2):
                           e.tensor_mul(o, a, b), inc="V")
                p.bank_consumed(6 + n % 2, "V", vm)

    normalize(kc, knb, 2)
    normalize(qc, qnb, 2)
    kn_v = p.cnt["V"]

    p.wait("sync", "V", kn_v)
    p.run("sync", lambda e, o=kn_hnd, i=knb.rearrange("q (k t) -> q k t", k=2),
          : e.dma_start(out=o.rearrange("(k q) t -> q k t", q=128), in_=i), inc="OQ0", dma=True)
    p.run("sync", lambda e, o=qn_hnd, i=qnb.rearrange("q (k t) -> q k t", k=2),
          : e.dma_start(out=o.rearrange("(k q) t -> q k t", q=128), in_=i), inc="OQ0", dma=True)

    for tt in range(16):
        tsl = slice(tt * 128, (tt + 1) * 128)
        scnt = None
        for j, srcb in enumerate((knb[:, 0 * T:1 * T], knb[:, 1 * T:2 * T],
                                  vc[:, 0 * T:1 * T], vc[:, 1 * T:2 * T],
                                  vc[:, 2 * T:3 * T], vc[:, 3 * T:4 * T])):
            pb = j % 2
            p.bank_wait("tensor", pb)
            p.wait("tensor", "V", kn_v)
            tc = p.run("tensor", lambda e, o=p.bank(pb)[:, 0:128], l=srcb[:, tsl],
                       r=iden_sb[:, :]: e.matmul(o, l, r, start=True, stop=True), inc="T")
            p.wait("scalar", "T", tc)
            p.wait("scalar", "TMQ", p.cnt.get("TMQ", 0))
            scnt = p.run("scalar", lambda e, o=tmst[:, j * 128:(j + 1) * 128],
                         i=p.bank(pb)[:, 0:128]: e.activation(o, i, AF.Copy), inc="S")
            p.bank_consumed(pb, "S", scnt)
        p.wait("sync", "S", scnt)
        p.run("sync", lambda e, o=kntmd[tsl, :], i=tmst[:, 0:KD1]:
              e.dma_start(out=o, in_=i), inc="TMQ", dma=True)
        p.run("sync", lambda e, o=vtmd[tsl, :], i=tmst[:, KD1:KD1 + VD1]:
              e.dma_start(out=o, in_=i), inc="TMQ", dma=True)

    p.wait("sync", "OQ0", p.cnt["OQ0"])
    p.wait("sync", "GQ", p.cnt["GQ"])
    p.wait("sync", "DCQ", p.cnt["DCQ"])
    p.wait("sync", "TMQ", p.cnt["TMQ"])
    p.wait("sync", "LG", p.cnt["LG"])


SCAN_CHUNK = 64


def build_e88_scan(p):
    nc = p.nc
    kn_hnd = nc.dram_tensor("kn_hnd", [KD1, T], BF16, kind="ExternalInput").ap()
    qn_hnd = nc.dram_tensor("qn_hnd", [KD1, T], BF16, kind="ExternalInput").ap()
    kntmd = nc.dram_tensor("kntmd", [T, KD1], BF16, kind="ExternalInput").ap()
    vtmd = nc.dram_tensor("vtmd", [T, VD1], BF16, kind="ExternalInput").ap()
    dcold = nc.dram_tensor("dcold", [2 * 128, T], BF16, kind="ExternalInput").ap()
    sqd = nc.dram_tensor("sqd", [VD1, T + 8], BF16, kind="ExternalOutput").ap()

    KNa = [p.sb(f"KNa{g}", [128, 4 * T], BF16) for g in range(2)]
    QNa = [p.sb(f"QNa{g}", [128, 4 * T], BF16) for g in range(2)]
    Dcol = [p.sb(f"Dcol{g}", [128, T], F32) for g in range(2)]
    V36 = [p.sb(f"V36_{i}", [36, SCAN_CHUNK * 128], BF16) for i in range(2)]
    KNT = [p.sb(f"KNT_{i}", [36, SCAN_CHUNK * 128], BF16) for i in range(2)]
    SqB = [p.sb(f"SqB_{i}", [100, SCAN_CHUNK * 64], BF16) for i in range(2)]
    S0 = p.sb("S0", [128, 128], BF16)
    S1 = p.sb("S1", [128, 128], BF16)
    DELTA = p.sb("DELTA", [36, 128], BF16)
    U = p.sb("U", [128, 128], F32)

    for g in range(2):
        p.run("vector", lambda e, o=KNa[g][:, :]: e.memset(o, 0.0), inc="V")
        p.run("vector", lambda e, o=QNa[g][:, :]: e.memset(o, 0.0), inc="V")
    p.run("vector", lambda e, o=V36[0][:, :]: e.memset(o, 0.0), inc="V")
    p.run("vector", lambda e, o=V36[1][:, :]: e.memset(o, 0.0), inc="V")
    p.run("vector", lambda e, o=KNT[0][:, :]: e.memset(o, 0.0), inc="V")
    p.run("vector", lambda e, o=KNT[1][:, :]: e.memset(o, 0.0), inc="V")
    p.run("vector", lambda e, o=S0[:, :]: e.memset(o, 0.0), inc="V")
    p.run("vector", lambda e, o=S1[:, :]: e.memset(o, 0.0), inc="V")
    vz = p.run("vector", lambda e, o=U[:, :]: e.memset(o, 0.0), inc="V")
    p.wait("sync", "V", vz)
    for g in range(2):
        for hp in range(4):
            p.run("sync", lambda e,
                  o=KNa[g][hp * 32:(hp + 1) * 32, hp * T:(hp + 1) * T],
                  i=kn_hnd[g * 128 + hp * 32:g * 128 + (hp + 1) * 32, :]:
                  e.dma_start(out=o, in_=i), inc="LQ", dma=True)
            p.run("sync", lambda e,
                  o=QNa[g][hp * 32:(hp + 1) * 32, hp * T:(hp + 1) * T],
                  i=qn_hnd[g * 128 + hp * 32:g * 128 + (hp + 1) * 32, :]:
                  e.dma_start(out=o, in_=i), inc="LQ", dma=True)
        p.run("gpsimd", lambda e, o=Dcol[g][:, :],
              i=dcold[g * 128:(g + 1) * 128, :]:
              e.dma_start(out=o, in_=i), inc="LG", dma=True)
    lq = p.cnt["LQ"]
    lg = p.cnt["LG"]
    p.wait("tensor", "LQ", lq)
    p.wait("vector", "LG", lg)
    p.wait("vector", "LQ", lq)

    import os as _os
    TSTEPS = int(_os.environ.get("E88_STEPS", str(T)))
    SMAX = int(_os.environ.get("E88_SMAX", "9"))
    NCH = T // SCAN_CHUNK

    def fill_chunk(ci, buf):
        t0 = ci * SCAN_CHUNK
        for g in range(2):
            p.run("sync", lambda e,
                  o=bass.AP(V36[buf].tensor, (g * 32) * (SCAN_CHUNK * 128) + g * 64,
                            [[SCAN_CHUNK * 128, 4], [128, SCAN_CHUNK], [1, 64]]),
                  i=bass.AP(vtmd.tensor, t0 * VD1 + g * 256,
                            [[64, 4], [VD1, SCAN_CHUNK], [1, 64]]),
                  : e.dma_start(out=o, in_=i), inc=f"VQ{buf}", dma=True)
            for hp in range(4):
                p.run("sync", lambda e,
                      o=bass.AP(KNT[buf].tensor,
                                (g * 32 + hp) * (SCAN_CHUNK * 128) + hp * 32,
                                [[SCAN_CHUNK * 128, 1], [128, SCAN_CHUNK], [1, 32]]),
                      i=bass.AP(kntmd.tensor, t0 * KD1 + g * 128 + hp * 32,
                                [[SCAN_CHUNK * KD1, 1], [KD1, SCAN_CHUNK], [1, 32]]),
                      : e.dma_start(out=o, in_=i), inc=f"VQ{buf}", dma=True)
        return p.cnt[f"VQ{buf}"]

    vq = [0, 0]
    sq_dma = [0, 0]
    vq[0] = fill_chunk(0, 0)
    vq[1] = fill_chunk(1, 1)

    pA = p.bank(0)
    pB = p.bank(1)
    pB2 = p.bank(2)

    tanh_s = 0
    evac_s = [0, 0, 0, 0]
    sub_v = 0
    stt_v = 0
    mm2_t = 0
    mma_t = 0

    Scur, Snxt = S0, S1
    for t in range(TSTEPS):
        ci, tl = t // SCAN_CHUNK, t % SCAN_CHUNK
        buf = ci % 2
        t4 = t % 4
        pcol = t4 * 128
        p.wait("tensor", "S", tanh_s)
        if t % 4 == 0:
            p.wait("tensor", "S", evac_s[-1])
        mma_t = None
        for ti, (src_, coff) in enumerate((
                (KNa[0], 0), (KNa[1], 32),
                (QNa[0], 64), (QNa[1], 96))):
            if src_ is KNa[0] or src_ is KNa[1]:
                g = 0 if src_ is KNa[0] else 1
                lsl = bass.AP(src_.tensor, t, [[4 * T, 128], [T, 4]])
            else:
                g = 0 if src_ is QNa[0] else 1
                tq = t - 1
                if tq < 0:
                    tq = 0
                lsl = bass.AP(src_.tensor, tq, [[4 * T, 128], [T, 4]])
            mma_t = p.run("tensor", lambda e, o=pA[coff:coff + 4, pcol:pcol + 128],
                          l=lsl, r=Scur[:, :], tp=(0, coff):
                          e.matmul(o, l, r, start=True, stop=True, tile_position=tp),
                          inc="T" if ti == 3 else None)
        if SMAX < 2:
            continue
        p.wait("vector", "T", mma_t)
        p.wait("vector", f"VQ{buf}", vq[buf])
        p.wait("vector", "V", stt_v)
        p.wait("vector", "T", mm2_t)
        p.run("vector", lambda e, o=DELTA[0:4, 0:64],
              a=V36[buf][0:4, tl * 128:tl * 128 + 64],
              b=pA[0:4, pcol:pcol + 64]:
              e.tensor_sub(o, a, b), inc="V")
        sub_v = p.run("vector", lambda e, o=DELTA[32:36, 64:128],
                      a=V36[buf][32:36, tl * 128 + 64:(tl + 1) * 128],
                      b=pA[32:36, pcol + 64:pcol + 128]:
                      e.tensor_sub(o, a, b), inc="V")
        if SMAX < 3:
            continue
        p.wait("tensor", "V", sub_v)
        p.run("tensor", lambda e, o=pB[:, 0:64],
              l=KNT[buf][0:4, tl * 128:(tl + 1) * 128], r=DELTA[0:4, 0:64]:
              e.matmul(o, l, r, start=True, stop=True), inc=None)
        mm2_t = p.run("tensor", lambda e, o=pB2[:, 0:64],
                      l=KNT[buf][32:36, tl * 128:(tl + 1) * 128], r=DELTA[32:36, 64:128]:
                      e.matmul(o, l, r, start=True, stop=True), inc="T")
        if SMAX < 4:
            continue
        p.wait("vector", "T", mm2_t)
        p.wait("vector", "S", tanh_s)
        for g in range(2):
            stt_v = p.run("vector", lambda e, o=U[:, g * 64:(g + 1) * 64],
                          i=Scur[:, g * 64:(g + 1) * 64], s=Dcol[g][:, t:t + 1],
                          b=(pB if g == 0 else pB2)[:, 0:64]:
                          e.scalar_tensor_tensor(o, i, s, b, ALU.mult, ALU.add), inc="V")
        if SMAX < 5:
            continue
        p.wait("scalar", "V", stt_v)
        tanh_s = p.run("scalar", lambda e, o=Snxt[:, :], i=U[:, :]:
                       e.activation(o, i, AF.Tanh), inc="S")
        Scur, Snxt = Snxt, Scur
        if t4 == 3 and SMAX >= 6:
            rnd = (t // 4) % (SCAN_CHUNK // 4)
            p.wait("scalar", "T", mma_t)
            p.wait("scalar", f"SQ{buf}", sq_dma[buf])
            p.run("scalar", lambda e,
                  o=bass.AP(SqB[buf].tensor, 64 * (SCAN_CHUNK * 64) + rnd * 4,
                            [[SCAN_CHUNK * 64, 4], [SCAN_CHUNK, 64], [1, 4]]),
                  i=bass.AP(pA.tensor, 64 * 512, [[512, 4], [1, 64], [128, 4]]),
                  : e.activation(o, i, AF.Copy), inc="S")
            ev = p.run("scalar", lambda e,
                       o=bass.AP(SqB[buf].tensor, 96 * (SCAN_CHUNK * 64) + rnd * 4,
                                 [[SCAN_CHUNK * 64, 4], [SCAN_CHUNK, 64], [1, 4]]),
                       i=bass.AP(pA.tensor, 96 * 512 + 64, [[512, 4], [1, 64], [128, 4]]),
                       : e.activation(o, i, AF.Copy), inc="S")
            evac_s = evac_s[1:] + [ev]
        if tl == SCAN_CHUNK - 1:
            ev = p.cnt["S"]
            p.wait("sync", "S", ev)
            t0 = ci * SCAN_CHUNK
            for g in range(2):
                sq_dma[buf] = p.run("sync", lambda e,
                    o=bass.AP(sqd.tensor, (g * 256) * (T + 8) + t0 + 3,
                              [[64 * (T + 8), 4], [T + 8, 64], [1, SCAN_CHUNK]]),
                    i=bass.AP(SqB[buf].tensor, (64 + g * 32) * (SCAN_CHUNK * 64),
                              [[SCAN_CHUNK * 64, 4], [SCAN_CHUNK, 64], [1, SCAN_CHUNK]]),
                    : e.dma_start(out=o, in_=i), inc=f"SQ{buf}", dma=True)
            if ci + 2 < NCH:
                p.wait("sync", "V", p.cnt["V"])
                vq[buf] = fill_chunk(ci + 2, buf)

    p.wait("tensor", "S", tanh_s)
    p.wait("tensor", "S", evac_s[-1])
    fin = []
    for ti, (src_, coff) in enumerate(((QNa[0], 64), (QNa[1], 96))):
        lsl = bass.AP(src_.tensor, T - 1, [[4 * T, 128], [T, 4]])
        tcf = p.run("tensor", lambda e, o=pA[coff:coff + 4, 0:128],
                    l=lsl, r=Scur[:, :], tp=(0, coff):
                    e.matmul(o, l, r, start=True, stop=True, tile_position=tp),
                    inc="T")
        fin.append(tcf)
    p.wait("scalar", "T", fin[-1])
    p.wait("scalar", "SQ0", p.cnt.get("SQ0", 0))
    p.wait("scalar", "SQ1", p.cnt.get("SQ1", 0))
    sfin = []
    for g in range(2):
        sf = p.run("scalar", lambda e,
                   o=bass.AP(SqB[0].tensor, (64 + g * 32) * (SCAN_CHUNK * 64),
                             [[SCAN_CHUNK * 64, 4], [SCAN_CHUNK, 64]]),
                   i=bass.AP(pA.tensor, (64 + g * 32) * 512 + g * 64, [[512, 4], [1, 64]]),
                   : e.activation(o, i, AF.Copy), inc="S")
        sfin.append(sf)
    p.wait("sync", "S", sfin[-1])
    p.wait("sync", "SQ0", sq_dma[0])
    for g in range(2):
        gfin = p.run("sync", lambda e,
                     o=bass.AP(sqd.tensor, (g * 256) * (T + 8) + (T - 1) + 4,
                               [[64 * (T + 8), 4], [T + 8, 64], [1, 1]]),
                     i=bass.AP(SqB[0].tensor, (64 + g * 32) * (SCAN_CHUNK * 64),
                               [[SCAN_CHUNK * 64, 4], [SCAN_CHUNK, 64], [1, 1]]),
                     : e.dma_start(out=o, in_=i, allow_slow_non_contiguous=True),
                     inc="SQ0", dma=True)
    p.wait("sync", "SQ0", p.cnt["SQ0"])
    p.wait("sync", "SQ1", p.cnt.get("SQ1", 0))


def build_e88_post(p):
    nc = p.nc
    xin = nc.dram_tensor("xin", [D, T], F32, kind="ExternalInput").ap()
    sqd = nc.dram_tensor("sqd", [VD1, T + 8], BF16, kind="ExternalInput").ap()
    gated = nc.dram_tensor("gated", [VD1, T], BF16, kind="ExternalInput").ap()
    wo1T = nc.dram_tensor("wo1T", [VD1, D], F32, kind="ExternalInput").ap()
    onw = nc.dram_tensor("onw", [128, 1], F32, kind="ExternalInput").ap()
    blk2 = nc.dram_tensor("blk2", [128, 2], F32, kind="ExternalInput").ap()
    blk2T = nc.dram_tensor("blk2T", [2, 128], F32, kind="ExternalInput").ap()
    xout = nc.dram_tensor("xout", [D, T], F32, kind="ExternalOutput").ap()

    c = make_common(p)
    ln = LNBufs(p, "ln")

    SqT = p.sb("SqT", [128, 4 * T], BF16)
    gat = p.sb("gat", [128, 4 * T], BF16)
    outF = p.sb("outF", [128, 4 * T], BF16)
    ybuf = p.sb("ybuf", [128, NKT * T], BF16)
    onw_sb = p.sb("onw_sb", [128, 1], F32)
    b2 = p.sb("b2", [128, 2], BF16)
    b2T = p.sb("b2T", [2, 128], F32)
    rr = p.sb("rr", [2, T], F32)
    sg1 = p.sb("sg1", [128, 512], BF16)
    strips = [p.sb(f"strip{i}", [128, 4 * 128], BF16) for i in range(2)]
    outb = [p.sb(f"outb{i}", [128, 512], F32) for i in range(2)]

    p.run("sync", lambda e, o=SqT.rearrange("q (k t) -> q k t", k=4),
          i=sqd[:, 4:4 + T].rearrange("(k q) t -> q k t", q=128),
          : e.dma_start(out=o, in_=i), inc="LQ", dma=True)
    p.run("sync", lambda e, o=gat.rearrange("q (k t) -> q k t", k=4),
          i=gated.rearrange("(k q) t -> q k t", q=128),
          : e.dma_start(out=o, in_=i), inc="LQ", dma=True)
    lq = p.run("sync", lambda e, o=onw_sb[:, :], i=onw: e.dma_start(out=o, in_=i),
               inc="LQ", dma=True)
    p.run("gpsimd", lambda e, o=b2[:, :], i=blk2: e.dma_start(out=o, in_=i), inc="LG", dma=True)
    lb = p.run("sync", lambda e, o=b2T[:, :], i=blk2T: e.dma_start(out=o, in_=i),
               inc="LB", dma=True)
    p.wait("tensor", "LB", lb)
    lg = p.cnt["LG"]

    for kt in range(4):
        tb = slice(kt * T, (kt + 1) * T)
        for n in range(4):
            nsl = slice(n * 512, (n + 1) * 512)
            sl2 = slice(kt * T + n * 512, kt * T + n * 512 + 512)
            p.wait("scalar", "LQ", lq)
            p.wait("scalar", "V", p.cnt["V"])
            s1 = p.run("scalar", lambda e, o=sg1, i=SqT[:, sl2]:
                       e.activation(o, i, AF.Sigmoid), inc="S")
            p.wait("vector", "S", s1)
            v1 = p.run("vector", lambda e, o=outF[:, sl2], a=SqT[:, sl2], b=sg1:
                       e.tensor_mul(o, a, b), inc="V")
            p.wait("vector", "V", v1)
            v2 = p.run("vector", lambda e, o=outF[:, sl2], a=outF[:, sl2], b=SqT[:, sl2]:
                       e.tensor_mul(o, a, b), inc="V")
            p.wait("scalar", "V", v2)
            s2 = p.run("scalar", lambda e, o=sg1, i=outF[:, sl2]:
                       e.activation(o, i, AF.Square), inc="S")
            p.wait("tensor", "S", s2)
            p.wait("tensor", "LG", lg)
            pb = n % 2
            p.bank_wait("tensor", pb)
            tc = p.run("tensor", lambda e, o=p.bank(pb)[0:2, :], l=b2, r=sg1:
                       e.matmul(o, l, r, start=True, stop=True), inc="T")
            p.wait("vector", "T", tc)
            v3 = p.run("vector", lambda e, o=rr[:, nsl], i=p.bank(pb)[0:2, :]:
                       e.tensor_scalar_mul(o, i, 1.0 / 64.0), inc="V")
            p.wait("scalar", "V", v3)
            s3 = p.run("scalar", lambda e, o=rr[:, nsl], i=rr[:, nsl],
                       b=c["cc"]("rms_eps", 2): e.activation(o, i, AF.Sqrt, bias=b), inc="S")
            p.bank_consumed(pb, "S", s3)
            p.wait("vector", "S", s3)
            v4 = p.run("vector", lambda e, o=rr[:, nsl], i=rr[:, nsl]:
                       e.reciprocal(o, i), inc="V")
            p.wait("tensor", "V", v4)
            pb2 = 2 + n % 2
            p.bank_wait("tensor", pb2)
            tb2 = p.run("tensor", lambda e, o=p.bank(pb2), l=b2T, r=rr[:, nsl]:
                        e.matmul(o, l, r, start=True, stop=True), inc="T")
            p.wait("vector", "T", tb2)
            v5 = p.run("vector", lambda e, o=outF[:, sl2], a=outF[:, sl2], b=p.bank(pb2):
                       e.tensor_mul(o, a, b), inc="V")
            p.bank_consumed(pb2, "V", v5)
            p.wait("vector", "V", v5)
            v6 = p.run("vector", lambda e, o=outF[:, sl2], i=outF[:, sl2], s=onw_sb[:, 0:1]:
                       e.tensor_scalar_mul(o, i, s), inc="V")
            p.wait("scalar", "V", v6)
            s4 = p.run("scalar", lambda e, o=sg1, i=gat[:, sl2]:
                       e.activation(o, i, AF.Sigmoid), inc="S")
            p.wait("vector", "S", s4)
            p.run("vector", lambda e, o=outF[:, sl2], a=outF[:, sl2], b=sg1:
                  e.tensor_mul(o, a, b), inc="V")
    outf_v = p.cnt["V"]

    strip_state = {"free": [("T", 0), ("T", 0)]}
    for m in range(NKT):
        sb_i = m % 2
        s, cv = strip_state["free"][sb_i]
        p.wait("gpsimd", s, cv)
        srcw = wo1T[:, m * 128:(m + 1) * 128].rearrange("(k q) j -> q k j", q=128)
        dstw = strips[sb_i].rearrange("q (k j) -> q k j", k=4)
        g = p.run("gpsimd", lambda e, o=dstw, i=srcw: e.dma_start(out=o, in_=i),
                  inc=f"WQ{sb_i}", dma=True)
        p.wait("tensor", f"WQ{sb_i}", g)
        p.wait("tensor", "V", outf_v)
        tc = None
        for n in range(4):
            nsl = slice(n * 512, (n + 1) * 512)
            pb = 4 + n % 2
            p.bank_wait("tensor", pb)
            for k in range(4):
                tc = p.run("tensor", lambda e, o=p.bank(pb),
                      l=strips[sb_i][:, k * 128:(k + 1) * 128],
                      r=outF[:, k * T:(k + 1) * T][:, nsl], st=(k == 0), sp=(k == 3):
                      e.matmul(o, l, r, start=st, stop=sp),
                      inc="T" if k == 3 else None)
            p.wait("scalar", "T", tc)
            ssc = p.run("scalar", lambda e, o=ybuf[:, m * T:(m + 1) * T][:, nsl],
                        i=p.bank(pb): e.activation(o, i, AF.Copy), inc="S")
            p.bank_consumed(pb, "S", ssc)
        strip_state["free"][sb_i] = ("T", tc)
    y_s = p.cnt["S"]

    xs = XStream(p, "xld", xin)

    def src_y(k):
        return ybuf[:, k * T:(k + 1) * T], "S", y_s

    def apply_post(k, n, yt, ysname, ycnt, pa, pb_):
        sl = slice(n * 512, (n + 1) * 512)
        if n % 2 == 0:
            p.wait("sync", "V", p.cnt["V"])
            p.wait("sync", "S", p.cnt["S"])
            p.wait("sync", "T", p.cnt["T"])
            ap, sname, g, b = xs.load(k)
            apply_post.res = (ap, sname, g)
        rap, rs, rc = apply_post.res
        ob = outb[k % 2]
        p.wait("vector", ysname, ycnt)
        p.wait("vector", "V", apply_post.t_free)
        v1 = p.run("vector", lambda e, o=ln.t1, i=yt[:, sl], b=pa:
                   e.tensor_mul(o, i, b), inc="V")
        p.wait("vector", "V", v1)
        v2 = p.run("vector", lambda e, o=ln.t2, i=ln.t1, b=pb_:
                   e.tensor_sub(o, i, b), inc="V")
        p.wait("vector", rs, rc)
        p.wait("vector", f"OQ{k % 2}", apply_post.ob_free.get(k % 2, 0))
        p.wait("vector", "V", v2)
        v3 = p.run("vector", lambda e, o=ob[:, :], a=ln.t2, b=rap[:, sl]:
                   e.tensor_add(o, a, b), inc="V")
        apply_post.t_free = v3
        p.wait("sync", "V", v3)
        dcnt = p.run("sync", lambda e, o=xout[k * 128:(k + 1) * 128, sl], i=ob[:, :]:
                     e.dma_start(out=o, in_=i), inc=f"OQ{k % 2}", dma=True)
        apply_post.ob_free[k % 2] = dcnt

    apply_post.t_free = 0
    apply_post.ob_free = {}
    apply_post.res = None

    emit_ln(p, c, ln, src_y, apply_post, src_bf16=True)
    p.wait("sync", "OQ0", p.cnt.get("OQ0", 0))
    p.wait("sync", "OQ1", p.cnt.get("OQ1", 0))


def e88_in_maps(x_bt, W):
    def cw_prep(w, ntile):
        a = np.asarray(w, np.float32).reshape(ntile, 128, 4)
        return np.ascontiguousarray(a.transpose(1, 0, 2).reshape(128, ntile * 4))

    blk4 = np.zeros((128, 4), np.float32)
    for pp in range(128):
        blk4[pp, pp // 32] = 1.0
    e8 = np.zeros((8, 2 * 128), np.float32)
    for g in range(2):
        for pp in range(128):
            e8[g * 4 + pp // 32, g * 128 + pp] = 1.0
    blk2 = np.zeros((128, 2), np.float32)
    for pp in range(128):
        blk2[pp, pp // 64] = 1.0
    com1 = {
        "wq1T": _t(W["Wq"]), "wk1T": _t(W["Wk"]), "wv1T": _t(W["Wv"]),
        "wg1T": _t(W["Wg"]), "wa1T": _t(W["Wa"]),
        "alog": np.asarray(W["A_log"], np.float32).reshape(8, 1),
        "dtb": np.asarray(W["dt_bias"], np.float32).reshape(8, 1),
        "qcw": cw_prep(W["q_conv_w"], 2), "kcw": cw_prep(W["k_conv_w"], 2),
        "vcw": cw_prep(W["v_conv_w"], 4),
        "blk4": blk4, "blk4T": np.ascontiguousarray(blk4.T),
        "e8g": e8, "iden": np.eye(128, dtype=np.float32),
    }
    com3 = {
        "wo1T": _t(W["Wo"]),
        "onw": np.ascontiguousarray(np.tile(np.asarray(W["o_norm_w"], np.float32), 2)
                                    .reshape(128, 1)),
        "blk2": blk2, "blk2T": np.ascontiguousarray(blk2.T),
    }
    m1 = [dict(com1, xin=_t(x_bt[b])) for b in range(x_bt.shape[0])]
    m2 = [dict() for b in range(x_bt.shape[0])]
    m3 = [dict(com3, xin=_t(x_bt[b])) for b in range(x_bt.shape[0])]
    return m1, m2, m3
